# version 08
# baseline (speedup 1.0000x reference)
import math
import numpy as np
from contextlib import ExitStack

import concourse.bass as bass
import concourse.mybir as mybir

F32 = mybir.dt.float32
BF16 = mybir.dt.bfloat16
AF = mybir.ActivationFunctionType
ALU = mybir.AluOpType
AX = mybir.AxisListType

DM = 4096
S = 2048
NT = 1024
H = 32
HKV = 8
HD = 128
NFC = DM // 128
EPS_RMS = 1e-6
EPS_LN = 1e-5
NEG_BIG = -1e30
RSQ_HD = 1.0 / math.sqrt(HD)


def alibi_slopes(n_heads):
    start = 2 ** (-(2 ** (-(math.log2(n_heads) - 3))))
    return [start * start ** i for i in range(n_heads)]


def host_consts(half):
    c = {}
    c["IDENT"] = np.eye(128, dtype=np.float32)
    c["ONES1"] = np.ones((1, 128), dtype=np.float32)
    c["ONESCOL"] = np.ones((128, 1), dtype=np.float32)
    c["SPOS"] = (np.arange(128, dtype=np.float32)[:, None]
                 + 128.0 * np.arange(16, dtype=np.float32)[None, :])
    c["BIASBLK"] = np.broadcast_to(
        16.0 * np.arange(128, dtype=np.float32) + 15.0, (128, 128)).copy()
    ql = np.arange(128)[:, None]
    t = np.arange(256)[None, :]
    c["MASKDIAG"] = np.where(t > 2 * ql + half, NEG_BIG, 0.0).astype(np.float32)
    sl = np.arange(128)[:, None]
    jq = np.arange(512)[None, :]
    c["MASKR"] = np.stack([
        np.where(128 * k + sl > 2 * jq + half, NEG_BIG, 0.0) for k in range(8)
    ]).astype(np.float32)
    er = np.arange(128)[None, :]
    e = np.arange(8)[:, None]
    c["E8"] = (er // 16 == e).astype(np.float32)
    return c


def stage_inputs(inputs, core):
    b, half = core // 2, core % 2
    x = np.asarray(inputs["L0_attn_x"])
    Wq = np.asarray(inputs["L0_attn_Wq"])
    Wk = np.asarray(inputs["L0_attn_Wk"])
    Wv = np.asarray(inputs["L0_attn_Wv"])
    Wo = np.asarray(inputs["L0_attn_Wo"])
    qw = np.asarray(inputs["L0_attn_q_norm_w"])
    kw = np.asarray(inputs["L0_attn_k_norm_w"])
    Wb = np.asarray(inputs["L0_ffn_W_base"])
    bb = np.asarray(inputs["L0_ffn_b_base"])
    Wr = np.asarray(inputs["L0_ffn_W_router"])
    A = np.asarray(inputs["L0_ffn_A_stack"])
    B = np.asarray(inputs["L0_ffn_B_stack"])

    d = {}
    xt = np.ascontiguousarray(x[b].T)
    d["XT"] = xt
    d["XO"] = np.ascontiguousarray(xt[:, half::2])
    d["WQT"] = np.ascontiguousarray(Wq.T.reshape(DM, H, HD).transpose(1, 0, 2))
    d["WKT"] = np.ascontiguousarray(Wk.T.reshape(DM, HKV, HD).transpose(1, 0, 2))
    d["WVT"] = np.ascontiguousarray(Wv.T.reshape(DM, HKV, HD).transpose(1, 0, 2))
    d["WOT"] = np.ascontiguousarray(
        Wo.T.reshape(H, HD, NFC, 128).transpose(2, 0, 1, 3))
    d["WBT"] = np.ascontiguousarray(
        Wb.T.reshape(NFC, 128, NFC, 128).transpose(2, 0, 1, 3))
    d["WRT"] = np.ascontiguousarray(Wr.T)
    d["AT"] = np.ascontiguousarray(A.reshape(128, DM).T)
    d["BT"] = np.ascontiguousarray(B.transpose(0, 2, 1).reshape(128, DM))
    d["BBIAS"] = np.ascontiguousarray(bb.reshape(NFC, 128).T)
    d["QNW"] = np.ascontiguousarray(qw.reshape(128, 1).astype(np.float32))
    d["KNW"] = np.ascontiguousarray(kw.reshape(128, 1).astype(np.float32))
    d.update(host_consts(half))
    return d


class Chan:
    def __init__(self, st, nc, name):
        self.sem = st.enter_context(nc.semaphore(name))
        self.n = 0
        self.lastw = {}


class Buf:
    def __init__(self):
        self.ready = None
        self.free = None

    def acquire(self, P, eng):
        if self.free is not None:
            P.waitv(eng, *self.free)

    def produced(self, chan):
        self.ready = (chan, chan.n)

    def wait_ready(self, P, eng):
        if self.ready is not None:
            P.waitv(eng, *self.ready)

    def consumed(self, chan):
        self.free = (chan, chan.n)


class Prog:
    ENGS = ("tensor", "vector", "scalar", "sync", "gpsimd")

    def __init__(self, nc, st):
        self.nc = nc
        self.st = st
        self.ops = {e: [] for e in self.ENGS}
        self._chans = {}

    def chan(self, name):
        if name not in self._chans:
            self._chans[name] = Chan(self.st, self.nc, name)
        return self._chans[name]

    def waitv(self, eng, chan, v):
        if v <= 0 or chan.lastw.get(eng, -1) >= v:
            return
        chan.lastw[eng] = v
        sem = chan.sem
        self.ops[eng].append(lambda e, s=sem, vv=v: e.wait_ge(s, vv))

    def wait(self, eng, chan):
        self.waitv(eng, chan, chan.n)

    def op(self, eng, meth, *args, inc=None, inck=1, **kwargs):
        if inc is not None:
            inc.n += inck
            sem = inc.sem
            def f(e, m=meth, a=args, kw=kwargs, s=sem, k=inck):
                getattr(e, m)(*a, **kw).then_inc(s, k)
        else:
            def f(e, m=meth, a=args, kw=kwargs):
                getattr(e, m)(*a, **kw)
        self.ops[eng].append(f)

    def finalize(self):
        with self.nc.Block() as block:
            @block.tensor
            def _(t):
                for f in self.ops["tensor"]:
                    f(t)

            @block.vector
            def _(v):
                for f in self.ops["vector"]:
                    f(v)

            @block.scalar
            def _(s):
                for f in self.ops["scalar"]:
                    f(s)

            @block.sync
            def _(s):
                for f in self.ops["sync"]:
                    f(s)

            @block.gpsimd
            def _(g):
                for f in self.ops["gpsimd"]:
                    f(g)


def build(upto=4, debug=()):
    nc = bass.Bass("TRN2", target_bir_lowering=False)
    st = ExitStack()
    dbg = set(debug)

    di = lambda name, shape: nc.dram_tensor(name, shape, F32, kind="ExternalInput")
    XT = di("XT", [DM, S])
    XO = di("XO", [DM, NT])
    WQT = di("WQT", [H, DM, HD])
    WKT = di("WKT", [HKV, DM, HD])
    WVT = di("WVT", [HKV, DM, HD])
    WOT = di("WOT", [NFC, H, HD, 128])
    WBT = di("WBT", [NFC, NFC, 128, 128])
    WRT = di("WRT", [DM, 8])
    AT = di("AT", [DM, 128])
    BT = di("BT", [128, DM])
    BBIAS = di("BBIAS", [128, NFC])
    QNW = di("QNW", [128, 1])
    KNW = di("KNW", [128, 1])
    IDENT = di("IDENT", [128, 128])
    ONES1 = di("ONES1", [1, 128])
    ONESCOL = di("ONESCOL", [128, 1])
    SPOS = di("SPOS", [128, 16])
    BIASBLK = di("BIASBLK", [128, 128])
    MASKDIAG = di("MASKDIAG", [128, 256])
    MASKR = di("MASKR", [8, 128, 512])
    E8 = di("E8", [8, 128])

    OUT = nc.dram_tensor("OUT", [DM, NT], F32, kind="ExternalOutput")
    QS = nc.dram_tensor("QS", [H, HD, NT], BF16, kind="Internal")
    X1PRE = nc.dram_tensor("X1PRE", [NFC, 128, NT], BF16, kind="Internal")
    X1F = nc.dram_tensor("X1F", [NFC, 128, NT], BF16, kind="Internal")
    O2PRE = nc.dram_tensor("O2PRE", [NFC, 128, NT], BF16, kind="Internal")

    dbg_t = {}
    if "kv" in dbg:
        dbg_t["KD"] = nc.dram_tensor("KD", [HD, HKV * S], BF16, kind="ExternalOutput")
        dbg_t["VD"] = nc.dram_tensor("VD", [128, HKV * 16 * HD], BF16, kind="ExternalOutput")
    if "q" in dbg:
        dbg_t["QD"] = nc.dram_tensor("QD", [HD, H * NT], BF16, kind="ExternalOutput")
    if "o" in dbg:
        dbg_t["OD"] = nc.dram_tensor("OD", [HD, H * NT], BF16, kind="ExternalOutput")
    if "x1" in dbg:
        dbg_t["X1D"] = nc.dram_tensor("X1D", [NFC, 128, NT], F32, kind="ExternalOutput")
    if "p" in dbg:
        dbg_t["PD"] = nc.dram_tensor("PD", [8, 128, 512], BF16, kind="ExternalOutput")
        dbg_t["UD"] = nc.dram_tensor("UD", [1, 512], BF16, kind="ExternalOutput")

    sb = lambda name, shape, dt: st.enter_context(nc.sbuf_tensor(name, shape, dt))
    XBIG = sb("XBIG", [128, NFC * 1024], BF16)
    OBIG = XBIG
    KBIG = sb("KBIG", [128, HKV * S], BF16)
    VBIG = sb("VBIG", [128, HKV * 16 * HD], BF16)
    WBUF = sb("WBUF", [128, 2 * DM], BF16)
    QBUF = sb("QBUF", [128, 2 * 1024], BF16)
    PTB = sb("PTB", [128, 3 * 512], BF16)
    XRES = sb("XRES", [128, 2 * 1024], F32)
    STAGE = sb("STAGE", [128, 2 * 1024], F32)
    SQB = sb("SQB", [128, 2 * 1024], BF16)

    identb = sb("identb", [128, 128], BF16)
    identf = sb("identf", [128, 128], F32)
    ones1b = sb("ones1b", [1, 128], BF16)
    ones1f = sb("ones1f", [1, 128], F32)
    onescolb = sb("onescolb", [128, 1], BF16)
    onescolf = sb("onescolf", [128, 1], F32)
    sposs = sb("sposs", [128, 16], F32)
    biasblk = sb("biasblk", [128, 128], F32)
    maskdiag = sb("maskdiag", [128, 256], BF16)
    maskr = sb("maskr", [128, 8 * 512], BF16)
    e8b = sb("e8b", [8, 128], BF16)
    qnw = sb("qnw", [128, 1], F32)
    knw = sb("knw", [128, 1], F32)
    wqwk = sb("wqwk", [128, 1], F32)
    bbias = sb("bbias", [128, NFC], F32)
    epsc = sb("epsc", [128, 2], F32)
    zeroc = sb("zeroc", [128, 1], F32)
    bm_sp = sb("bm_sp", [128, 2 * 16], F32)
    bm_blk = sb("bm_blk", [128, 2 * 128], F32)
    bmaxt = sb("bmaxt", [128, 2 * 128], F32)
    utmp = sb("utmp", [128, 128], F32)
    negUc = sb("negUc", [128, 8], BF16)
    sqs = sb("sqs", [128, 2 * 512], BF16)
    lnrow = sb("lnrow", [1, 5 * 512], F32)
    arow = sb("arow", [128, 1024], F32)
    brow = sb("brow", [128, 1024], F32)
    rt = sb("rt", [128, 32], F32)
    rcol = sb("rcol", [128, 8], F32)
    ctmp = sb("ctmp", [128, 8], F32)
    negUrow = SQB[0:1, 0:1024]
    invr_sb = SQB[0:1, 1024:2048]
    invrb_sb = sqs
    rowtmp = lnrow
    rinv_row = lnrow[0:1, 1024:1536]
    invb_sb = arow
    abuf = KBIG[:, 0:DM]
    bbuf = KBIG[:, DM:2 * DM]
    wrbuf = KBIG[:, 2 * DM:2 * DM + NFC * 8]
    xres4 = [KBIG[:, 8448:9472], KBIG[:, 9472:10496]]
    stout3 = [PTB[:, 0:1024], QBUF[:, 1024:2048]]
    stout4 = stout3
    lgsb = brow[0:8, :]
    crep = arow
    accx = arow
    accx2 = brow
    cemb = SQB[0:8, 0:1024]
    hcb = QBUF[:, 0:1024]

    pbank = [st.enter_context(nc.psum_tensor(f"pb{i}", [128, 512], F32))
             for i in range(7)]
    pbb = st.enter_context(nc.psum_tensor("pbb", [128, 1024], BF16))

    P = Prog(nc, st)
    T = P.chan("T")
    V = P.chan("V")
    SC = P.chan("SC")
    g, t, v, s, y = "gpsimd", "tensor", "vector", "scalar", "sync"
    slopes = alibi_slopes(H)

    CQ = P.chan("CQ")
    for dst, src in ((identb, IDENT), (identf, IDENT), (ones1b, ONES1),
                     (ones1f, ONES1), (onescolb, ONESCOL), (onescolf, ONESCOL),
                     (sposs, SPOS),
                     (biasblk, BIASBLK), (maskdiag, MASKDIAG), (e8b, E8),
                     (qnw, QNW), (knw, KNW), (bbias, BBIAS)):
        P.op(g, "dma_start", dst[:], src[:], inc=CQ, inck=16)
    P.op(g, "dma_start", maskr[:].rearrange("p (k c) -> p k c", c=512),
         MASKR[:].rearrange("k p c -> p k c"), inc=CQ, inck=16)
    consts_done = CQ.n

    P.waitv(v, CQ, consts_done)
    P.op(v, "memset", epsc[:, 0:1], EPS_RMS, inc=V)
    P.op(v, "memset", epsc[:, 1:2], EPS_LN, inc=V)
    P.op(v, "memset", zeroc[:], 0.0, inc=V)
    P.op(v, "scalar_tensor_tensor", wqwk[:], qnw[:], RSQ_HD, knw[:],
         op0=ALU.mult, op1=ALU.mult, inc=V)
    init_v = V.n

    XQ = P.chan("XQ")
    WQCH = [P.chan("WQ0"), P.chan("WQ1")]
    xbig_b = Buf()
    wb = [Buf(), Buf()]

    def wslice(par):
        return WBUF[:, par * DM:(par + 1) * DM]

    def load_w_head(src_ap, par, dsub=HD):
        wb[par].acquire(P, g)
        P.op(g, "dma_start",
             wslice(par).rearrange("p (a d) -> p a d", d=dsub),
             src_ap.rearrange("(a p) d -> p a d", p=128),
             inc=WQCH[par], inck=16)
        wb[par].produced(WQCH[par])

    pk_b = [Buf(), Buf()]
    pv_b = [Buf(), Buf()]
    pst_b = Buf()
    pbc_b = Buf()
    pbb_b = Buf()
    sqs_b = [Buf(), Buf()]
    invb_b = [Buf(), Buf()]
    inv1_b = Buf()
    sqbv_b = [Buf(), Buf()]
    wseq = 0

    def rms_inv_chain(pnum, sg):
        sqs_b[sg].acquire(P, s)
        P.op(s, "activation", sqs[:, sg * 512:sg * 512 + 512],
             pbank[pnum][:], AF.Square, inc=SC)
        sqs_b[sg].produced(SC)
        sqs_b[sg].wait_ready(P, t)
        pst_b.acquire(P, t)
        P.op(t, "matmul", pbank[5][0:1, :], onescolb[:],
             sqs[:, sg * 512:sg * 512 + 512], start=True, stop=True, inc=T)
        pst_b.produced(T)
        sqs_b[sg].consumed(T)
        pst_b.wait_ready(P, s)
        inv1_b.acquire(P, s)
        P.op(s, "activation", rowtmp[0:1, 0:512], pbank[5][0:1, :],
             AF.Ln, bias=epsc[0:1, 0:1], scale=1.0 / HD, inc=SC)
        P.wait(s, SC)
        P.op(s, "activation", rinv_row, rowtmp[0:1, 0:512],
             AF.Exp, bias=zeroc[0:1, 0:1], scale=-0.5, inc=SC)
        inv1_b.produced(SC)
        pst_b.consumed(SC)
        inv1_b.wait_ready(P, t)
        pbc_b.acquire(P, t)
        P.op(t, "matmul", pbank[6][:], ones1f[:], rinv_row,
             start=True, stop=True, inc=T)
        pbc_b.produced(T)
        inv1_b.consumed(T)
        pbc_b.wait_ready(P, s)
        invb_b[sg].acquire(P, s)
        P.op(s, "copy", invb_sb[:, sg * 512:sg * 512 + 512], pbank[6][:], inc=SC)
        invb_b[sg].produced(SC)
        pbc_b.consumed(SC)

    for sh in range(2):
        xbig_b.acquire(P, g)
        for fc in range(NFC):
            P.op(g, "dma_start", XBIG[:, fc * 1024:(fc + 1) * 1024],
                 XT[fc * 128:(fc + 1) * 128, sh * 1024:(sh + 1) * 1024],
                 inc=XQ, inck=16)
        xbig_b.produced(XQ)
        for kv in range(HKV):
            for wtyp, WSRC in ((0, WKT), (1, WVT)):
                par = wseq % 2
                load_w_head(WSRC[kv], par)
                wseq += 1
                for sg in range(2):
                    s0 = sh * 1024 + sg * 512
                    xbig_b.wait_ready(P, t)
                    wb[par].wait_ready(P, t)
                    P.waitv(t, V, init_v)
                    pbx = (pk_b if wtyp == 0 else pv_b)[sg]
                    pnum = (0 if wtyp == 0 else 2) + sg
                    pbx.acquire(P, t)
                    for fc in range(NFC):
                        P.op(t, "matmul", pbank[pnum][:],
                             wslice(par)[:, fc * HD:(fc + 1) * HD],
                             XBIG[:, fc * 1024 + sg * 512: fc * 1024 + sg * 512 + 512],
                             start=(fc == 0), stop=(fc == NFC - 1),
                             inc=(T if fc == NFC - 1 else None))
                    pbx.produced(T)
                    if wtyp == 0:
                        pbx.wait_ready(P, s)
                        rms_inv_chain(pnum, sg)
                        invb_b[sg].wait_ready(P, v)
                        pbx.wait_ready(P, v)
                        P.op(v, "tensor_tensor",
                             KBIG[:, kv * S + s0: kv * S + s0 + 512],
                             pbank[pnum][:], invb_sb[:, sg * 512:sg * 512 + 512],
                             op=ALU.mult, inc=V)
                        pbx.consumed(V)
                        invb_b[sg].consumed(V)
                    else:
                        pbx.wait_ready(P, v)
                        sqbv_b[sg].acquire(P, v)
                        P.op(v, "tensor_copy", SQB[:, sg * 1024:sg * 1024 + 512],
                             pbank[pnum][:], inc=V)
                        sqbv_b[sg].produced(V)
                        pbx.consumed(V)
                        sqbv_b[sg].wait_ready(P, t)
                        pbb_b.acquire(P, t)
                        for ts4 in range(4):
                            P.op(t, "transpose",
                                 pbb[:, ts4 * 128:(ts4 + 1) * 128],
                                 SQB[:, sg * 1024 + ts4 * 128: sg * 1024 + ts4 * 128 + 128],
                                 identb[:], inc=(T if ts4 == 3 else None))
                        pbb_b.produced(T)
                        sqbv_b[sg].consumed(T)
                        pbb_b.wait_ready(P, v)
                        base = (kv * 16 + s0 // 128) * 128
                        P.op(v, "tensor_copy", VBIG[:, base:base + 512],
                             pbb[:, 0:512], inc=V)
                        pbb_b.consumed(V)
                wb[par].consumed(T)
        xbig_b.consumed(T)

    kv_done_v = V.n

    QWCH = [P.chan("QW0"), P.chan("QW1")]
    if upto >= 1.5:
        xbig_b.acquire(P, g)
        for fc in range(NFC):
            P.op(g, "dma_start", XBIG[:, fc * 1024:(fc + 1) * 1024],
                 XO[fc * 128:(fc + 1) * 128, :], inc=XQ, inck=16)
        xbig_b.produced(XQ)
        stage_b = [Buf(), Buf()]
        qbs_b = [Buf(), Buf()]
        for h in range(H):
            par = wseq % 2
            load_w_head(WQT[h], par)
            wseq += 1
            hq = h % 2
            qbs_b[hq].acquire(P, s)
            for qg in range(2):
                xbig_b.wait_ready(P, t)
                wb[par].wait_ready(P, t)
                pq_b = pk_b[qg]
                pnum = qg
                pq_b.acquire(P, t)
                for fc in range(NFC):
                    P.op(t, "matmul", pbank[pnum][:],
                         wslice(par)[:, fc * HD:(fc + 1) * HD],
                         XBIG[:, fc * 1024 + qg * 512: fc * 1024 + qg * 512 + 512],
                         start=(fc == 0), stop=(fc == NFC - 1),
                         inc=(T if fc == NFC - 1 else None))
                pq_b.produced(T)
                pq_b.wait_ready(P, s)
                rms_inv_chain(pnum, qg)
                invb_b[qg].wait_ready(P, v)
                pq_b.wait_ready(P, v)
                stage_b[qg].acquire(P, v)
                P.op(v, "tensor_tensor", STAGE[:, qg * 512:qg * 512 + 512],
                     pbank[pnum][:], invb_sb[:, qg * 512:qg * 512 + 512],
                     op=ALU.mult, inc=V)
                stage_b[qg].produced(V)
                pq_b.consumed(V)
                invb_b[qg].consumed(V)
                stage_b[qg].wait_ready(P, s)
                P.op(s, "activation",
                     QBUF[:, hq * 1024 + qg * 512: hq * 1024 + qg * 512 + 512],
                     STAGE[:, qg * 512:qg * 512 + 512], AF.Copy,
                     scale=wqwk[:], bias=0.0, inc=SC)
                stage_b[qg].consumed(SC)
            qbs_b[hq].produced(SC)
            qbs_b[hq].wait_ready(P, y)
            P.op(y, "dma_start", QS[h], QBUF[:, hq * 1024:hq * 1024 + 1024],
                 inc=QWCH[hq], inck=16)
            qbs_b[hq].consumed(QWCH[hq])
            wb[par].consumed(T)
        xbig_b.consumed(T)
    qs_done = (QWCH[0].n, QWCH[1].n)

    dch = P.chan("DBG")
    if "kv" in dbg:
        P.waitv(y, V, kv_done_v)
        P.op(y, "dma_start", dbg_t["KD"][:], KBIG[:], inc=dch, inck=16)
        P.op(y, "dma_start", dbg_t["VD"][:], VBIG[:], inc=dch, inck=16)
    if "q" in dbg and upto >= 1.5:
        P.waitv(y, QWCH[0], qs_done[0])
        P.waitv(y, QWCH[1], qs_done[1])
        P.op(y, "dma_start", dbg_t["QD"][:].rearrange("d (h n) -> d h n", n=NT),
             QS[:].rearrange("h d n -> d h n"), inc=dch, inck=16)

    obig_done = 0
    if upto >= 2:
        P.waitv(t, V, kv_done_v)
        P.waitv(y, QWCH[0], qs_done[0])
        P.waitv(y, QWCH[1], qs_done[1])
        QLCH = [P.chan("QL0"), P.chan("QL1")]
        qld_b = [Buf(), Buf()]
        bm_b = [Buf(), Buf()]
        po_b = [Buf(), Buf()]
        pr_b = [Buf(), Buf()]
        pp_b = [Buf(), Buf(), Buf()]
        negUc_b = [Buf(), Buf()]
        negu_b = [Buf(), Buf()]
        negub_b = [Buf(), Buf()]
        negUb = STAGE[:, 1024:2048]
        defer_fin = []
        invr1_b = [Buf(), Buf()]
        invrb_b = [Buf(), Buf()]
        pt_b = [Buf(), Buf(), Buf()]
        poolidx = 0

        def pool_next():
            nonlocal poolidx
            i = poolidx % 3
            poolidx += 1
            return pp_b[i], pbank[i]

        for kv in range(HKV):
            for gq in range(4):
                h = kv * 4 + gq
                hq = h % 2
                qld_b[hq].acquire(P, y)
                P.op(y, "dma_start", QBUF[:, hq * 1024:hq * 1024 + 1024],
                     QS[h], inc=QLCH[hq], inck=16)
                qld_b[hq].produced(QLCH[hq])
                bm_b[hq].acquire(P, v)
                P.op(v, "tensor_scalar", bm_sp[:, hq * 16:hq * 16 + 16],
                     sposs[:], float(slopes[h]), None, op0=ALU.mult)
                P.op(v, "tensor_scalar", bm_blk[:, hq * 128:hq * 128 + 128],
                     biasblk[:], float(slopes[h]), None, op0=ALU.mult, inc=V)
                bm_b[hq].produced(V)
                for m in range(2):
                    negUc_b[m].acquire(P, v)
                    for jt in range(4):
                        j = 4 * m + jt
                        L = 256 * (j + 1)
                        nch = (L + 511) // 512
                        for ci in range(nch):
                            c0 = ci * 512
                            Nc = min(512, L - c0)
                            pb_, pps = pool_next()
                            pb_.acquire(P, t)
                            qld_b[hq].wait_ready(P, t)
                            last = (ci == nch - 1)
                            P.op(t, "matmul", pps[:, 0:Nc],
                                 QBUF[:, hq * 1024 + j * 128: hq * 1024 + j * 128 + 128],
                                 KBIG[:, kv * S + c0: kv * S + c0 + Nc],
                                 start=True, stop=not last,
                                 inc=(None if last else T))
                            if last:
                                doff = (L - 256) - c0
                                P.op(t, "matmul", pps[:, doff:doff + 256],
                                     identb[:], maskdiag[:],
                                     start=False, stop=True, inc=T)
                            pb_.produced(T)
                            pb_.wait_ready(P, v)
                            P.wait(v, V)
                            nb = Nc // 16
                            P.op(v, "tensor_reduce",
                                 bmaxt[:, m * 128 + c0 // 16: m * 128 + c0 // 16 + nb],
                                 pps[:, 0:Nc].rearrange("p (a b) -> p a b", b=16),
                                 axis=AX.X, op=ALU.max, inc=V)
                            pb_.consumed(V)
                        nblk = 16 * (j + 1)
                        bm_b[hq].wait_ready(P, v)
                        P.wait(v, V)
                        P.op(v, "tensor_tensor", utmp[:, 0:nblk],
                             bmaxt[:, m * 128:m * 128 + nblk],
                             bm_blk[:, hq * 128:hq * 128 + nblk],
                             op=ALU.add, inc=V)
                        P.wait(v, V)
                        P.op(v, "tensor_reduce", negUc[:, m * 4 + jt:m * 4 + jt + 1],
                             utmp[:, 0:nblk], axis=AX.X, op=ALU.max,
                             negate=True, inc=V)
                    negUc_b[m].produced(V)
                for m in range(2):
                    negUc_b[m].wait_ready(P, t)
                    pbb_b.acquire(P, t)
                    for jt in range(4):
                        P.op(t, "transpose", pbb[0:1, jt * 128:jt * 128 + 128],
                             negUc[:, m * 4 + jt:m * 4 + jt + 1], identb[:],
                             inc=(T if jt == 3 else None))
                    pbb_b.produced(T)
                    negUc_b[m].consumed(T)
                    pbb_b.wait_ready(P, s)
                    negu_b[m].acquire(P, s)
                    P.op(s, "copy", negUrow[0:1, m * 512:m * 512 + 512],
                         pbb[0:1, 0:512], inc=SC)
                    negu_b[m].produced(SC)
                    pbb_b.consumed(SC)
                for m in range(2):
                    iu = m
                    St = 8 * (m + 1)
                    negu_b[m].wait_ready(P, t)
                    pb_, pps = pool_next()
                    pb_.acquire(P, t)
                    P.op(t, "matmul", pps[:], ones1b[:],
                         negUrow[0:1, m * 512:m * 512 + 512],
                         start=True, stop=True, inc=T)
                    pb_.produced(T)
                    negu_b[m].consumed(T)
                    pb_.wait_ready(P, s)
                    negub_b[m].acquire(P, s)
                    P.op(s, "copy", negUb[:, m * 512:m * 512 + 512], pps[:],
                         inc=SC)
                    negub_b[m].produced(SC)
                    pb_.consumed(SC)
                    po_b[iu].acquire(P, t)
                    pr_b[iu].acquire(P, t)
                    pend = []

                    def emit_qk(sidx, m=m, iu=iu, hq=hq, kv=kv, pend=pend):
                        kp = sidx - 8 * m
                        pb_, pps = pool_next()
                        pb_.acquire(P, t)
                        P.op(t, "matmul", pps[:],
                             KBIG[:, kv * S + sidx * 128: kv * S + sidx * 128 + 128],
                             QBUF[:, hq * 1024 + m * 512: hq * 1024 + m * 512 + 512],
                             start=True, stop=(kp < 0),
                             inc=(None if kp >= 0 else T))
                        if kp >= 0:
                            P.op(t, "matmul", pps[:], identb[:],
                                 maskr[:, kp * 512:kp * 512 + 512],
                                 start=False, stop=True, inc=T)
                        pb_.produced(T)
                        pb_.wait_ready(P, v)
                        negub_b[m].wait_ready(P, v)
                        P.op(v, "tensor_tensor", pps[:], pps[:],
                             negUb[:, m * 512:m * 512 + 512], op=ALU.add, inc=V)
                        pb_.produced(V)
                        ptb_ = pt_b[sidx % 3]
                        ptb_.acquire(P, s)
                        pb_.wait_ready(P, s)
                        bm_b[hq].wait_ready(P, s)
                        P.op(s, "activation",
                             PTB[:, (sidx % 3) * 512:(sidx % 3) * 512 + 512],
                             pps[:], AF.Exp,
                             bias=bm_sp[:, hq * 16 + sidx: hq * 16 + sidx + 1],
                             scale=1.0, inc=SC)
                        ptb_.produced(SC)
                        pb_.consumed(SC)
                        pend.append((sidx, ptb_))

                    def emit_av(m=m, iu=iu, kv=kv, St=St, pend=pend):
                        sidx, ptb_ = pend.pop(0)
                        ptb_.wait_ready(P, t)
                        vb = (kv * 16 + sidx) * 128
                        P.op(t, "matmul", pbank[3 + iu][:],
                             VBIG[:, vb:vb + 128],
                             PTB[:, (sidx % 3) * 512:(sidx % 3) * 512 + 512],
                             start=(sidx == 0), stop=(sidx == St - 1))
                        P.op(t, "matmul", pbank[5 + iu][0:1, :],
                             onescolb[:],
                             PTB[:, (sidx % 3) * 512:(sidx % 3) * 512 + 512],
                             start=(sidx == 0), stop=(sidx == St - 1),
                             inc=(T if sidx == St - 1 else None))
                        ptb_.consumed(T)

                    for sidx in range(St):
                        emit_qk(sidx)
                        if sidx == 2:
                            while defer_fin:
                                defer_fin.pop(0)()
                        if len(pend) >= 3:
                            emit_av()
                    while pend:
                        emit_av()
                    po_b[iu].produced(T)
                    pr_b[iu].produced(T)
                    negub_b[m].consumed(V)

                    def finish_om(h=h, m=m, iu=iu):
                        pr_b[iu].wait_ready(P, s)
                        invr1_b[iu].acquire(P, s)
                        P.op(s, "activation",
                             rowtmp[0:1, iu * 512:iu * 512 + 512],
                             pbank[5 + iu][0:1, :], AF.Ln,
                             bias=zeroc[0:1, 0:1], scale=1.0, inc=SC)
                        P.wait(s, SC)
                        P.op(s, "activation",
                             invr_sb[0:1, iu * 512:iu * 512 + 512],
                             rowtmp[0:1, iu * 512:iu * 512 + 512], AF.Exp,
                             bias=zeroc[0:1, 0:1], scale=-1.0, inc=SC)
                        invr1_b[iu].produced(SC)
                        pr_b[iu].consumed(SC)
                        invr1_b[iu].wait_ready(P, t)
                        pb_, pps = pool_next()
                        pb_.acquire(P, t)
                        P.op(t, "matmul", pps[:], ones1b[:],
                             invr_sb[0:1, iu * 512:iu * 512 + 512],
                             start=True, stop=True, inc=T)
                        pb_.produced(T)
                        invr1_b[iu].consumed(T)
                        pb_.wait_ready(P, s)
                        invrb_b[iu].acquire(P, s)
                        P.op(s, "copy", invrb_sb[:, iu * 512:iu * 512 + 512],
                             pps[:], inc=SC)
                        invrb_b[iu].produced(SC)
                        pb_.consumed(SC)
                        invrb_b[iu].wait_ready(P, v)
                        po_b[iu].wait_ready(P, v)
                        P.op(v, "tensor_tensor",
                             OBIG[:, h * 1024 + m * 512: h * 1024 + m * 512 + 512],
                             pbank[3 + iu][:],
                             invrb_sb[:, iu * 512:iu * 512 + 512],
                             op=ALU.mult, inc=V)
                        po_b[iu].consumed(V)
                        invrb_b[iu].consumed(V)

                    defer_fin.append(finish_om)
                bm_b[hq].consumed(SC)
                qld_b[hq].consumed(T)
        while defer_fin:
            defer_fin.pop(0)()
        obig_done = V.n
        if "o" in dbg:
            P.waitv(y, V, obig_done)
            P.op(y, "dma_start", dbg_t["OD"][:], OBIG[:], inc=dch, inck=16)
            od_dump = dch.n

    xres_b = [Buf(), Buf()]
    stg_b = [Buf(), Buf()]
    stout_b3 = [Buf(), Buf()]
    stout_b4 = stout_b3
    XRCH = [P.chan("XR0"), P.chan("XR1")]
    SWCH = [P.chan("SW0"), P.chan("SW1")]
    x1_done_sc = 0
    acc_free_T = 0
    if upto >= 3:
        P.waitv(t, V, obig_done)
        P.waitv(t, SC, SC.n)
        psum_o = [Buf(), Buf(), Buf(), Buf()]
        sqb_q = [[Buf(), Buf()], [Buf(), Buf()]]
        pend_sq = []

        def flush_sq():
            oc_, par_, qg_ = pend_sq.pop(0)
            b = sqb_q[par_][qg_]
            b.wait_ready(P, v)
            slot = SQB[:, par_ * 1024 + qg_ * 512: par_ * 1024 + qg_ * 512 + 512]
            dst = accx2[:, qg_ * 512:qg_ * 512 + 512]
            if oc_ == 0:
                P.op(v, "tensor_copy", dst, slot, inc=V)
            else:
                P.op(v, "tensor_tensor", dst, dst, slot, op=ALU.add, inc=V)
            b.consumed(V)

        def ln_block(oc, par, qg, psum_buf, pnum, resid_stt, first_wait=0,
                     resid=None, resid_b=None):
            if resid is None:
                resid = XRES[:, par * 1024:par * 1024 + 1024]
            if resid_b is None:
                resid_b = xres_b[par]
            psum_buf.wait_ready(P, v)
            resid_b.wait_ready(P, v)
            sl = STAGE[:, par * 1024 + qg * 512: par * 1024 + qg * 512 + 512]
            if resid_stt is None:
                P.op(v, "tensor_tensor", sl, pbank[pnum][:],
                     resid[:, qg * 512:qg * 512 + 512],
                     op=ALU.add, inc=V)
            else:
                P.op(v, "scalar_tensor_tensor", sl, pbank[pnum][:],
                     resid_stt,
                     resid[:, qg * 512:qg * 512 + 512],
                     op0=ALU.add, op1=ALU.add, inc=V)
            psum_buf.consumed(V)
            if qg == 1:
                resid_b.consumed(V)
            stv = V.n
            dstx = accx[:, qg * 512:qg * 512 + 512]
            if oc == 0:
                P.waitv(v, T, first_wait)
                P.op(v, "tensor_copy", dstx, sl, inc=V)
            else:
                P.op(v, "tensor_tensor", dstx, dstx, sl, op=ALU.add, inc=V)
            sqb_q[par][qg].acquire(P, s)
            P.waitv(s, V, stv)
            P.op(s, "activation",
                 SQB[:, par * 1024 + qg * 512: par * 1024 + qg * 512 + 512],
                 sl, AF.Square, inc=SC)
            sqb_q[par][qg].produced(SC)
            pend_sq.append((oc, par, qg))
            while len(pend_sq) > 2:
                flush_sq()

        def ln_stats_reduce():
            while pend_sq:
                flush_sq()
            P.wait(t, V)
            for qg in range(2):
                P.op(t, "matmul", pbank[3 + qg][0:1, :], onescolf[:],
                     accx[:, qg * 512:qg * 512 + 512],
                     start=True, stop=True, inc=T)
                P.op(t, "matmul", pbank[5 + qg][0:1, :], onescolf[:],
                     accx2[:, qg * 512:qg * 512 + 512],
                     start=True, stop=True, inc=T)

        def ln_finalize():
            ME, ES, VA, AA, BB = 0, 512, 1024, 1536, 2048
            P.wait(s, T)
            for qg in range(2):
                P.op(s, "activation", lnrow[0:1, ME:ME + 512],
                     pbank[3 + qg][0:1, :], AF.Copy, scale=1.0 / DM,
                     bias=0.0, inc=SC)
                P.op(s, "activation", lnrow[0:1, ES:ES + 512],
                     pbank[5 + qg][0:1, :], AF.Copy, scale=1.0 / DM,
                     bias=0.0, inc=SC)
                P.wait(v, SC)
                P.op(v, "tensor_tensor", lnrow[0:1, VA:VA + 512],
                     lnrow[0:1, ME:ME + 512], lnrow[0:1, ME:ME + 512],
                     op=ALU.mult, inc=V)
                P.wait(v, V)
                P.op(v, "tensor_tensor", lnrow[0:1, VA:VA + 512],
                     lnrow[0:1, ES:ES + 512], lnrow[0:1, VA:VA + 512],
                     op=ALU.subtract, inc=V)
                P.wait(s, V)
                P.op(s, "activation", lnrow[0:1, AA:AA + 512],
                     lnrow[0:1, VA:VA + 512], AF.Ln,
                     bias=epsc[0:1, 1:2], scale=1.0, inc=SC)
                P.wait(s, SC)
                P.op(s, "activation", lnrow[0:1, AA:AA + 512],
                     lnrow[0:1, AA:AA + 512], AF.Exp,
                     bias=zeroc[0:1, 0:1], scale=-0.5, inc=SC)
                P.wait(v, SC)
                P.op(v, "tensor_tensor", lnrow[0:1, BB:BB + 512],
                     lnrow[0:1, ME:ME + 512], lnrow[0:1, AA:AA + 512],
                     op=ALU.mult, inc=V)
                P.wait(v, V)
                P.op(v, "tensor_scalar", lnrow[0:1, BB:BB + 512],
                     lnrow[0:1, BB:BB + 512], -1.0, None, op0=ALU.mult, inc=V)
                P.wait(t, V)
                P.op(t, "matmul", pbank[qg][:], ones1f[:],
                     lnrow[0:1, AA:AA + 512], start=True, stop=True)
                P.op(t, "matmul", pbank[2][:] if qg == 0 else pbank[5][:],
                     ones1f[:], lnrow[0:1, BB:BB + 512],
                     start=True, stop=True, inc=T)
                P.wait(s, T)
                P.op(s, "copy", arow[:, qg * 512:qg * 512 + 512], pbank[qg][:])
                P.op(s, "copy", brow[:, qg * 512:qg * 512 + 512],
                     pbank[2][:] if qg == 0 else pbank[5][:], inc=SC)

        for oc in range(NFC):
            par = oc % 2
            wb[par].acquire(P, g)
            P.op(g, "dma_start",
                 wslice(par).rearrange("p (a d) -> p a d", d=128),
                 WOT[oc].rearrange("a p d -> p a d"), inc=WQCH[par], inck=16)
            wb[par].produced(WQCH[par])
            xres_b[par].acquire(P, y)
            P.op(y, "dma_start", XRES[:, par * 1024:par * 1024 + 1024],
                 XO[oc * 128:(oc + 1) * 128, :], inc=XRCH[par], inck=16)
            xres_b[par].produced(XRCH[par])
            stg_b[par].acquire(P, v)
            for qg in range(2):
                pnum = 2 * (oc % 2) + qg
                pob = psum_o[pnum]
                pob.acquire(P, t)
                wb[par].wait_ready(P, t)
                for hh in range(H):
                    P.op(t, "matmul", pbank[pnum][:],
                         wslice(par)[:, hh * 128:hh * 128 + 128],
                         OBIG[:, hh * 1024 + qg * 512: hh * 1024 + qg * 512 + 512],
                         start=(hh == 0), stop=(hh == H - 1),
                         inc=(T if hh == H - 1 else None))
                pob.produced(T)
                if qg == 1:
                    wb[par].consumed(T)
                ln_block(oc, par, qg, pob, pnum, None)
            stg_b[par].produced(V)
            stg_b[par].wait_ready(P, s)
            stout_b3[par].acquire(P, s)
            P.op(s, "copy", stout3[par], STAGE[:, par * 1024:par * 1024 + 1024],
                 inc=SC)
            stout_b3[par].produced(SC)
            stg_b[par].consumed(SC)
            stout_b3[par].wait_ready(P, y)
            P.op(y, "dma_start", X1PRE[oc], stout3[par], inc=SWCH[par], inck=16)
            stout_b3[par].consumed(SWCH[par])
        if "o" in dbg:
            P.waitv(s, dch, od_dump)
        ln_stats_reduce()
        acc_free_T = T.n
        ln_finalize()
        ln1_ab = SC.n
        P.waitv(y, SWCH[0], SWCH[0].n)
        P.waitv(y, SWCH[1], SWCH[1].n)
        P.waitv(g, SWCH[0], SWCH[0].n)
        P.waitv(g, SWCH[1], SWCH[1].n)
        P.waitv(y, SC, ln1_ab)
        vb_b = [Buf() for _ in range(8)]
        VAPL = [P.chan("VAP0"), P.chan("VAP1")]

        def apply_read(oc, SRC):
            sl8 = oc % 8
            eng = y if oc % 2 == 0 else g
            ch = VAPL[oc % 2]
            vb_b[sl8].acquire(P, eng)
            P.op(eng, "dma_start", VBIG[:, sl8 * 1024:sl8 * 1024 + 1024],
                 SRC[oc], inc=ch, inck=16)
            vb_b[sl8].produced(ch)

        for oc in range(6):
            apply_read(oc, X1PRE)
        for oc in range(NFC):
            sl8 = oc % 8
            par = oc % 2
            vb_b[sl8].wait_ready(P, v)
            P.waitv(v, SC, ln1_ab)
            stg_b[par].acquire(P, v)
            P.op(v, "tensor_tensor", STAGE[:, par * 1024:par * 1024 + 1024],
                 VBIG[:, sl8 * 1024:sl8 * 1024 + 1024], arow[:], op=ALU.mult,
                 inc=V)
            P.wait(v, V)
            P.op(v, "tensor_tensor", STAGE[:, par * 1024:par * 1024 + 1024],
                 STAGE[:, par * 1024:par * 1024 + 1024], brow[:], op=ALU.add,
                 inc=V)
            stg_b[par].produced(V)
            vb_b[sl8].consumed(V)
            if oc + 6 < NFC:
                apply_read(oc + 6, X1PRE)
            stg_b[par].wait_ready(P, s)
            P.op(s, "copy", XBIG[:, oc * 1024:(oc + 1) * 1024],
                 STAGE[:, par * 1024:par * 1024 + 1024], inc=SC)
            stg_b[par].consumed(SC)
            P.waitv(y, SC, SC.n)
            P.op(y, "dma_start", X1F[oc], XBIG[:, oc * 1024:(oc + 1) * 1024],
                 inc=SWCH[par], inck=16)
        x1_done_sc = SC.n
        if "x1" in dbg:
            P.waitv(y, SWCH[0], SWCH[0].n)
            P.waitv(y, SWCH[1], SWCH[1].n)
            P.op(y, "dma_start",
                 dbg_t["X1D"][:].rearrange("a p n -> p a n"),
                 X1F[:].rearrange("a p n -> p a n"), inc=dch, inck=16)

    ODC = [P.chan("OD0"), P.chan("OD1")]
    if upto >= 4:
        P.waitv(t, SC, x1_done_sc)
        FQ = P.chan("FQ")
        P.waitv(g, V, obig_done)
        P.op(g, "dma_start", abuf.rearrange("p (a d) -> p a d", d=128),
             AT.rearrange("(a p) d -> p a d", p=128), inc=FQ, inck=16)
        P.op(g, "dma_start", bbuf, BT[:], inc=FQ, inck=16)
        P.op(g, "dma_start", wrbuf.rearrange("p (a d) -> p a d", d=8),
             WRT.rearrange("(a p) d -> p a d", p=128), inc=FQ, inck=16)
        ffnw = FQ.n
        P.waitv(v, FQ, ffnw)
        P.op(v, "tensor_scalar", bbuf, bbuf, 2.0, None, op0=ALU.mult, inc=V)
        bscale_v = V.n

        P.waitv(t, FQ, ffnw)
        for qg in range(2):
            if qg == 1:
                P.wait(t, SC)
            for fc in range(NFC):
                P.op(t, "matmul", pbank[0][0:8, :],
                     wrbuf[:, fc * 8:fc * 8 + 8],
                     XBIG[:, fc * 1024 + qg * 512: fc * 1024 + qg * 512 + 512],
                     start=(fc == 0), stop=(fc == NFC - 1),
                     inc=(T if fc == NFC - 1 else None))
            P.wait(s, T)
            P.op(s, "copy", lgsb[:, qg * 512:qg * 512 + 512], pbank[0][0:8, :],
                 inc=SC)
        P.wait(t, SC)
        for tt8 in range(8):
            P.op(t, "transpose", pbank[1][0:128, tt8 * 8:tt8 * 8 + 8],
                 lgsb[:, tt8 * 128:tt8 * 128 + 128], identf[0:8, 0:8],
                 inc=(T if tt8 == 7 else None))
        tr_done = T.n
        P.waitv(v, T, tr_done)
        for tt8 in range(8):
            if tt8 > 0:
                P.wait(v, T)
            P.op(v, "tensor_copy", rt[:, 0:8], pbank[1][0:128, tt8 * 8:tt8 * 8 + 8],
                 inc=V)
            P.wait(v, V)
            P.op(v, "reduce_max", rcol[:, 0:1], rt[:, 0:8], axis=AX.X, inc=V)
            P.wait(v, V)
            P.op(v, "tensor_scalar", rt[:, 8:16], rt[:, 0:8], rcol[:, 0:1], None,
                 op0=ALU.is_ge, inc=V)
            P.wait(v, V)
            P.op(v, "scalar_tensor_tensor", rt[:, 16:24], rt[:, 8:16], NEG_BIG,
                 rt[:, 0:8], op0=ALU.mult, op1=ALU.add, inc=V)
            P.wait(v, V)
            P.op(v, "reduce_max", rcol[:, 1:2], rt[:, 16:24], axis=AX.X, inc=V)
            P.op(v, "tensor_scalar", rcol[:, 2:3], rcol[:, 0:1], -1.0, None,
                 op0=ALU.mult, inc=V)
            P.wait(v, V)
            P.op(v, "tensor_scalar", rt[:, 24:32], rt[:, 16:24], rcol[:, 1:2],
                 None, op0=ALU.is_ge, inc=V)
            P.wait(s, V)
            P.op(s, "activation", rcol[:, 3:4], rcol[:, 1:2], AF.Exp,
                 bias=rcol[:, 2:3], scale=1.0, inc=SC)
            P.wait(v, SC)
            P.op(v, "tensor_scalar", rcol[:, 4:5], rcol[:, 3:4], 1.0, None,
                 op0=ALU.add, inc=V)
            P.wait(v, V)
            P.op(v, "reciprocal", rcol[:, 5:6], rcol[:, 4:5], inc=V)
            P.wait(v, V)
            P.op(v, "tensor_tensor", rcol[:, 6:7], rcol[:, 3:4], rcol[:, 5:6],
                 op=ALU.mult, inc=V)
            P.wait(v, V)
            P.op(v, "tensor_scalar", ctmp[:, 0:8], rt[:, 8:16], rcol[:, 5:6],
                 None, op0=ALU.mult, inc=V)
            P.wait(v, V)
            P.op(v, "scalar_tensor_tensor", ctmp[:, 0:8], rt[:, 24:32],
                 rcol[:, 6:7], ctmp[:, 0:8], op0=ALU.mult, op1=ALU.add, inc=V)
            P.wait(t, V)
            P.op(t, "transpose",
                 pbank[2 + tt8 // 4][0:8, (tt8 % 4) * 128:(tt8 % 4) * 128 + 128],
                 ctmp[:, 0:8], identf[:], inc=T)
        P.wait(s, T)
        P.op(s, "copy", cemb[:, 0:512], pbank[2][0:8, 0:512], inc=SC)
        P.op(s, "copy", cemb[:, 512:1024], pbank[3][0:8, 0:512], inc=SC)
        P.wait(t, SC)
        for qg in range(2):
            if qg == 1:
                P.wait(t, SC)
            P.op(t, "matmul", pbank[3][:], e8b[:],
                 cemb[:, qg * 512:qg * 512 + 512], start=True, stop=True, inc=T)
            P.wait(s, T)
            P.op(s, "copy", crep[:, qg * 512:qg * 512 + 512], pbank[3][:],
                 inc=SC)
        P.wait(t, SC)
        for qg in range(2):
            if qg == 1:
                P.wait(t, V)
            for fc in range(NFC):
                P.op(t, "matmul", pbank[4][:],
                     abuf[:, fc * 128:fc * 128 + 128],
                     XBIG[:, fc * 1024 + qg * 512: fc * 1024 + qg * 512 + 512],
                     start=(fc == 0), stop=(fc == NFC - 1),
                     inc=(T if fc == NFC - 1 else None))
            P.wait(v, T)
            P.waitv(v, SC, SC.n)
            P.op(v, "tensor_tensor", hcb[:, qg * 512:qg * 512 + 512],
                 pbank[4][:], crep[:, qg * 512:qg * 512 + 512], op=ALU.mult,
                 inc=V)
        hc_done = V.n

        psum_f = [Buf(), Buf(), Buf(), Buf()]
        xr4_b = [Buf(), Buf()]
        P.waitv(t, V, hc_done)
        P.waitv(t, V, bscale_v)
        P.waitv(y, SWCH[0], SWCH[0].n)
        P.waitv(y, SWCH[1], SWCH[1].n)
        for oc in range(NFC):
            par = oc % 2
            wb[par].acquire(P, g)
            P.op(g, "dma_start",
                 wslice(par).rearrange("p (a d) -> p a d", d=128),
                 WBT[oc].rearrange("a p d -> p a d"), inc=WQCH[par], inck=16)
            wb[par].produced(WQCH[par])
            xr4_b[par].acquire(P, y)
            P.op(y, "dma_start", xres4[par], X1F[oc], inc=XRCH[par], inck=16)
            xr4_b[par].produced(XRCH[par])
            stg_b[par].acquire(P, v)
            for qg in range(2):
                pnum = 2 * (oc % 2) + qg
                pfb = psum_f[pnum]
                pfb.acquire(P, t)
                wb[par].wait_ready(P, t)
                for fc in range(NFC):
                    P.op(t, "matmul", pbank[pnum][:],
                         wslice(par)[:, fc * 128:fc * 128 + 128],
                         XBIG[:, fc * 1024 + qg * 512: fc * 1024 + qg * 512 + 512],
                         start=(fc == 0), stop=False)
                P.op(t, "matmul", pbank[pnum][:],
                     bbuf[:, oc * 128:oc * 128 + 128],
                     hcb[:, qg * 512:qg * 512 + 512],
                     start=False, stop=True, inc=T)
                pfb.produced(T)
                if qg == 1:
                    wb[par].consumed(T)
                ln_block(oc, par, qg, pfb, pnum, bbias[:, oc:oc + 1],
                         first_wait=acc_free_T,
                         resid=xres4[par], resid_b=xr4_b[par])
            stg_b[par].produced(V)
            stg_b[par].wait_ready(P, s)
            stout_b4[par].acquire(P, s)
            P.op(s, "copy", stout4[par], STAGE[:, par * 1024:par * 1024 + 1024],
                 inc=SC)
            stout_b4[par].produced(SC)
            stg_b[par].consumed(SC)
            stout_b4[par].wait_ready(P, y)
            P.op(y, "dma_start", O2PRE[oc], stout4[par], inc=SWCH[par], inck=16)
            stout_b4[par].consumed(SWCH[par])
        ln_stats_reduce()
        ME, ES, VA, AA, BB = 0, 512, 1024, 1536, 2048
        P.wait(s, T)
        for qg in range(2):
            P.op(s, "activation", lnrow[0:1, ME:ME + 512],
                 pbank[3 + qg][0:1, :], AF.Copy, scale=1.0 / DM,
                 bias=0.0, inc=SC)
            P.op(s, "activation", lnrow[0:1, ES:ES + 512],
                 pbank[5 + qg][0:1, :], AF.Copy, scale=1.0 / DM,
                 bias=0.0, inc=SC)
            P.wait(v, SC)
            P.op(v, "tensor_tensor", lnrow[0:1, VA:VA + 512],
                 lnrow[0:1, ME:ME + 512], lnrow[0:1, ME:ME + 512],
                 op=ALU.mult, inc=V)
            P.wait(v, V)
            P.op(v, "tensor_tensor", lnrow[0:1, VA:VA + 512],
                 lnrow[0:1, ES:ES + 512], lnrow[0:1, VA:VA + 512],
                 op=ALU.subtract, inc=V)
            P.wait(s, V)
            P.op(s, "activation", lnrow[0:1, AA:AA + 512],
                 lnrow[0:1, VA:VA + 512], AF.Ln,
                 bias=epsc[0:1, 1:2], scale=1.0, inc=SC)
            P.wait(s, SC)
            P.op(s, "activation", lnrow[0:1, AA:AA + 512],
                 lnrow[0:1, AA:AA + 512], AF.Exp,
                 bias=zeroc[0:1, 0:1], scale=-0.5, inc=SC)
            P.wait(v, SC)
            P.op(v, "tensor_tensor", lnrow[0:1, BB:BB + 512],
                 lnrow[0:1, ME:ME + 512], lnrow[0:1, AA:AA + 512],
                 op=ALU.mult, inc=V)
            P.wait(v, V)
            P.op(v, "tensor_scalar", lnrow[0:1, BB:BB + 512],
                 lnrow[0:1, BB:BB + 512], -1.0, None, op0=ALU.mult, inc=V)
            P.wait(t, V)
            P.op(t, "matmul", pbank[qg][:], ones1f[:],
                 lnrow[0:1, AA:AA + 512], start=True, stop=True)
            P.op(t, "matmul", pbank[2][:] if qg == 0 else pbank[5][:],
                 ones1f[:], lnrow[0:1, BB:BB + 512],
                 start=True, stop=True, inc=T)
            P.wait(s, T)
            P.op(s, "copy", arow[:, qg * 512:qg * 512 + 512], pbank[qg][:])
            P.op(s, "copy", brow[:, qg * 512:qg * 512 + 512],
                 pbank[2][:] if qg == 0 else pbank[5][:], inc=SC)
        ln2_ab = SC.n
        P.waitv(y, SWCH[0], SWCH[0].n)
        P.waitv(y, SWCH[1], SWCH[1].n)
        P.waitv(g, SWCH[0], SWCH[0].n)
        P.waitv(g, SWCH[1], SWCH[1].n)
        P.waitv(y, SC, ln2_ab)
        for oc in range(6):
            apply_read(oc, O2PRE)
        for oc in range(NFC):
            sl8 = oc % 8
            par = oc % 2
            vb_b[sl8].wait_ready(P, v)
            P.waitv(v, SC, ln2_ab)
            stg_b[par].acquire(P, v)
            P.op(v, "tensor_tensor", STAGE[:, par * 1024:par * 1024 + 1024],
                 VBIG[:, sl8 * 1024:sl8 * 1024 + 1024], arow[:], op=ALU.mult,
                 inc=V)
            P.wait(v, V)
            P.op(v, "tensor_tensor", STAGE[:, par * 1024:par * 1024 + 1024],
                 STAGE[:, par * 1024:par * 1024 + 1024], brow[:], op=ALU.add,
                 inc=V)
            stg_b[par].produced(V)
            vb_b[sl8].consumed(V)
            if oc + 6 < NFC:
                apply_read(oc + 6, O2PRE)
            stg_b[par].wait_ready(P, y)
            P.op(y, "dma_start", OUT[oc * 128:(oc + 1) * 128, :],
                 STAGE[:, par * 1024:par * 1024 + 1024], inc=ODC[par], inck=16)
            stg_b[par].consumed(ODC[par])

    if "OD0" in P._chans:
        P.wait(y, P._chans["OD0"])
        P.wait(y, P._chans["OD1"])
    for chn in ("SW0", "SW1", "QW0", "QW1"):
        if chn in P._chans:
            P.wait(y, P._chans[chn])
    if dch.n:
        P.wait(y, dch)
    P.finalize()
    return nc, st




from concourse.bass_utils import run_bass_kernel_spmd

_CACHE = {}


def kernel(**inputs):
    if "nc" not in _CACHE:
        _CACHE["nc"] = build(upto=4, debug=())
    nc, _st = _CACHE["nc"]
    in_maps = [stage_inputs(inputs, c) for c in range(8)]
    res = run_bass_kernel_spmd(nc, in_maps, core_ids=list(range(8)))
    out = np.empty((4, S, DM), dtype=np.float32)
    for c in range(8):
        b, half = c // 2, c % 2
        out[b, half::2, :] = np.asarray(res.results[c]["OUT"]).T
    return out


# version 09
# speedup vs baseline: 1.0842x; 1.0842x over previous
import math
import numpy as np
from contextlib import ExitStack

import concourse.bass as bass
import concourse.mybir as mybir

F32 = mybir.dt.float32
BF16 = mybir.dt.bfloat16
AF = mybir.ActivationFunctionType
ALU = mybir.AluOpType
AX = mybir.AxisListType

DM = 4096
S = 2048
NT = 1024
H = 32
HKV = 8
HD = 128
NFC = DM // 128
EPS_RMS = 1e-6
EPS_LN = 1e-5
NEG_BIG = -1e30
RSQ_HD = 1.0 / math.sqrt(HD)


def alibi_slopes(n_heads):
    start = 2 ** (-(2 ** (-(math.log2(n_heads) - 3))))
    return [start * start ** i for i in range(n_heads)]


def host_consts(half):
    c = {}
    c["IDENT"] = np.eye(128, dtype=np.float32)
    c["ONES1"] = np.ones((1, 128), dtype=np.float32)
    c["ONESCOL"] = np.ones((128, 1), dtype=np.float32)
    c["SPOS"] = (np.arange(128, dtype=np.float32)[:, None]
                 + 128.0 * np.arange(16, dtype=np.float32)[None, :])
    c["BIASBLK"] = np.broadcast_to(
        16.0 * np.arange(128, dtype=np.float32) + 15.0, (128, 128)).copy()
    ql = np.arange(128)[:, None]
    t = np.arange(256)[None, :]
    c["MASKDIAG"] = np.where(t > 2 * ql + half, NEG_BIG, 0.0).astype(np.float32)
    sl = np.arange(128)[:, None]
    jq = np.arange(512)[None, :]
    c["MASKR"] = np.stack([
        np.where(128 * k + sl > 2 * jq + half, NEG_BIG, 0.0) for k in range(8)
    ]).astype(np.float32)
    er = np.arange(128)[None, :]
    e = np.arange(8)[:, None]
    c["E8"] = (er // 16 == e).astype(np.float32)
    return c


def stage_inputs(inputs, core):
    b, half = core // 2, core % 2
    x = np.asarray(inputs["L0_attn_x"])
    Wq = np.asarray(inputs["L0_attn_Wq"])
    Wk = np.asarray(inputs["L0_attn_Wk"])
    Wv = np.asarray(inputs["L0_attn_Wv"])
    Wo = np.asarray(inputs["L0_attn_Wo"])
    qw = np.asarray(inputs["L0_attn_q_norm_w"])
    kw = np.asarray(inputs["L0_attn_k_norm_w"])
    Wb = np.asarray(inputs["L0_ffn_W_base"])
    bb = np.asarray(inputs["L0_ffn_b_base"])
    Wr = np.asarray(inputs["L0_ffn_W_router"])
    A = np.asarray(inputs["L0_ffn_A_stack"])
    B = np.asarray(inputs["L0_ffn_B_stack"])

    d = {}
    xt = np.ascontiguousarray(x[b].T)
    d["XT"] = xt
    d["XO"] = np.ascontiguousarray(xt[:, half::2])
    d["WQT"] = np.ascontiguousarray(Wq.T.reshape(DM, H, HD).transpose(1, 0, 2))
    d["WKT"] = np.ascontiguousarray(Wk.T.reshape(DM, HKV, HD).transpose(1, 0, 2))
    d["WVT"] = np.ascontiguousarray(Wv.T.reshape(DM, HKV, HD).transpose(1, 0, 2))
    d["WOT"] = np.ascontiguousarray(
        Wo.T.reshape(H, HD, NFC, 128).transpose(2, 0, 1, 3))
    d["WBT"] = np.ascontiguousarray(
        Wb.T.reshape(NFC, 128, NFC, 128).transpose(2, 0, 1, 3))
    d["WRT"] = np.ascontiguousarray(Wr.T)
    d["AT"] = np.ascontiguousarray(A.reshape(128, DM).T)
    d["BT"] = np.ascontiguousarray(B.transpose(0, 2, 1).reshape(128, DM))
    d["BBIAS"] = np.ascontiguousarray(bb.reshape(NFC, 128).T)
    d["QNW"] = np.ascontiguousarray(qw.reshape(128, 1).astype(np.float32))
    d["KNW"] = np.ascontiguousarray(kw.reshape(128, 1).astype(np.float32))
    d.update(host_consts(half))
    return d


class Chan:
    def __init__(self, st, nc, name):
        self.sem = st.enter_context(nc.semaphore(name))
        self.n = 0
        self.lastw = {}


class Buf:
    def __init__(self):
        self.ready = None
        self.free = None

    def acquire(self, P, eng):
        if self.free is not None:
            P.waitv(eng, *self.free)

    def produced(self, chan):
        self.ready = (chan, chan.n)

    def wait_ready(self, P, eng):
        if self.ready is not None:
            P.waitv(eng, *self.ready)

    def consumed(self, chan):
        self.free = (chan, chan.n)


class Prog:
    ENGS = ("tensor", "vector", "scalar", "sync", "gpsimd")

    def __init__(self, nc, st):
        self.nc = nc
        self.st = st
        self.ops = {e: [] for e in self.ENGS}
        self._chans = {}

    def chan(self, name):
        if name not in self._chans:
            self._chans[name] = Chan(self.st, self.nc, name)
        return self._chans[name]

    def waitv(self, eng, chan, v):
        if v <= 0 or chan.lastw.get(eng, -1) >= v:
            return
        chan.lastw[eng] = v
        sem = chan.sem
        self.ops[eng].append(lambda e, s=sem, vv=v: e.wait_ge(s, vv))

    def wait(self, eng, chan):
        self.waitv(eng, chan, chan.n)

    def op(self, eng, meth, *args, inc=None, inck=1, **kwargs):
        if inc is not None:
            inc.n += inck
            sem = inc.sem
            def f(e, m=meth, a=args, kw=kwargs, s=sem, k=inck):
                getattr(e, m)(*a, **kw).then_inc(s, k)
        else:
            def f(e, m=meth, a=args, kw=kwargs):
                getattr(e, m)(*a, **kw)
        self.ops[eng].append(f)

    def finalize(self):
        with self.nc.Block() as block:
            @block.tensor
            def _(t):
                for f in self.ops["tensor"]:
                    f(t)

            @block.vector
            def _(v):
                for f in self.ops["vector"]:
                    f(v)

            @block.scalar
            def _(s):
                for f in self.ops["scalar"]:
                    f(s)

            @block.sync
            def _(s):
                for f in self.ops["sync"]:
                    f(s)

            @block.gpsimd
            def _(g):
                for f in self.ops["gpsimd"]:
                    f(g)


def build(upto=4, debug=()):
    nc = bass.Bass("TRN2", target_bir_lowering=False)
    st = ExitStack()
    dbg = set(debug)

    di = lambda name, shape: nc.dram_tensor(name, shape, F32, kind="ExternalInput")
    XT = di("XT", [DM, S])
    XO = di("XO", [DM, NT])
    WQT = di("WQT", [H, DM, HD])
    WKT = di("WKT", [HKV, DM, HD])
    WVT = di("WVT", [HKV, DM, HD])
    WOT = di("WOT", [NFC, H, HD, 128])
    WBT = di("WBT", [NFC, NFC, 128, 128])
    WRT = di("WRT", [DM, 8])
    AT = di("AT", [DM, 128])
    BT = di("BT", [128, DM])
    BBIAS = di("BBIAS", [128, NFC])
    QNW = di("QNW", [128, 1])
    KNW = di("KNW", [128, 1])
    IDENT = di("IDENT", [128, 128])
    ONES1 = di("ONES1", [1, 128])
    ONESCOL = di("ONESCOL", [128, 1])
    SPOS = di("SPOS", [128, 16])
    BIASBLK = di("BIASBLK", [128, 128])
    MASKDIAG = di("MASKDIAG", [128, 256])
    MASKR = di("MASKR", [8, 128, 512])
    E8 = di("E8", [8, 128])

    OUT = nc.dram_tensor("OUT", [DM, NT], F32, kind="ExternalOutput")
    QS = nc.dram_tensor("QS", [H, HD, NT], BF16, kind="Internal")
    X1PRE = nc.dram_tensor("X1PRE", [NFC, 128, NT], BF16, kind="Internal")
    X1F = nc.dram_tensor("X1F", [NFC, 128, NT], BF16, kind="Internal")
    O2PRE = nc.dram_tensor("O2PRE", [NFC, 128, NT], BF16, kind="Internal")

    dbg_t = {}
    if "kv" in dbg:
        dbg_t["KD"] = nc.dram_tensor("KD", [HD, HKV * S], BF16, kind="ExternalOutput")
        dbg_t["VD"] = nc.dram_tensor("VD", [128, HKV * 16 * HD], BF16, kind="ExternalOutput")
    if "q" in dbg:
        dbg_t["QD"] = nc.dram_tensor("QD", [HD, H * NT], BF16, kind="ExternalOutput")
    if "o" in dbg:
        dbg_t["OD"] = nc.dram_tensor("OD", [HD, H * NT], BF16, kind="ExternalOutput")
    if "x1" in dbg:
        dbg_t["X1D"] = nc.dram_tensor("X1D", [NFC, 128, NT], F32, kind="ExternalOutput")
    if "p" in dbg:
        dbg_t["PD"] = nc.dram_tensor("PD", [8, 128, 512], BF16, kind="ExternalOutput")
        dbg_t["UD"] = nc.dram_tensor("UD", [1, 512], BF16, kind="ExternalOutput")

    sb = lambda name, shape, dt: st.enter_context(nc.sbuf_tensor(name, shape, dt))
    XBIG = sb("XBIG", [128, NFC * 1024], BF16)
    OBIG = XBIG
    KBIG = sb("KBIG", [128, HKV * S], BF16)
    VBIG = sb("VBIG", [128, HKV * 16 * HD], BF16)
    WBUF = sb("WBUF", [128, 2 * DM], BF16)
    QBUF = sb("QBUF", [128, 2 * 1024], BF16)
    PTB = sb("PTB", [128, 3 * 512], BF16)
    XRES = sb("XRES", [128, 2 * 1024], F32)
    STAGE = sb("STAGE", [128, 2 * 1024], F32)
    SQB = sb("SQB", [128, 2 * 1024], BF16)

    identb = sb("identb", [128, 128], BF16)
    identf = sb("identf", [128, 128], F32)
    ones1b = sb("ones1b", [1, 128], BF16)
    ones1f = sb("ones1f", [1, 128], F32)
    onescolb = sb("onescolb", [128, 1], BF16)
    onescolf = sb("onescolf", [128, 1], F32)
    sposs = sb("sposs", [128, 16], F32)
    biasblk = sb("biasblk", [128, 128], F32)
    maskdiag = sb("maskdiag", [128, 256], BF16)
    maskr = sb("maskr", [128, 8 * 512], BF16)
    e8b = sb("e8b", [8, 128], BF16)
    qnw = sb("qnw", [128, 1], F32)
    knw = sb("knw", [128, 1], F32)
    wqwk = sb("wqwk", [128, 1], F32)
    bbias = sb("bbias", [128, NFC], F32)
    epsc = sb("epsc", [128, 2], F32)
    zeroc = sb("zeroc", [128, 1], F32)
    bm_sp = sb("bm_sp", [128, 2 * 16], F32)
    bm_blk = sb("bm_blk", [128, 2 * 128], F32)
    bmaxt = sb("bmaxt", [128, 2 * 128], F32)
    utmp = sb("utmp", [128, 128], F32)
    negUc = sb("negUc", [128, 8], BF16)
    sqs = sb("sqs", [128, 2 * 512], BF16)
    lnrow = sb("lnrow", [1, 5 * 512], F32)
    arow = sb("arow", [128, 1024], F32)
    brow = sb("brow", [128, 1024], F32)
    rt = sb("rt", [128, 32], F32)
    rcol = sb("rcol", [128, 8], F32)
    ctmp = sb("ctmp", [128, 8], F32)
    negUrow = SQB[0:1, 0:1024]
    invr_sb = SQB[0:1, 1024:2048]
    invrb_sb = sqs
    rowtmp = lnrow
    rinv_row = lnrow[0:1, 1024:1536]
    invb_sb = arow
    abuf = KBIG[:, 0:DM]
    bbuf = KBIG[:, DM:2 * DM]
    wrbuf = KBIG[:, 2 * DM:2 * DM + NFC * 8]
    xres4 = [KBIG[:, 8448:9472], KBIG[:, 9472:10496]]
    stout3 = [PTB[:, 0:1024], QBUF[:, 1024:2048]]
    stout4 = stout3
    lgsb = brow[0:8, :]
    crep = arow
    accx = arow
    accx2 = brow
    cemb = SQB[0:8, 0:1024]
    hcb = QBUF[:, 0:1024]

    pbank = [st.enter_context(nc.psum_tensor(f"pb{i}", [128, 512], F32))
             for i in range(7)]
    pbb = st.enter_context(nc.psum_tensor("pbb", [128, 1024], BF16))

    P = Prog(nc, st)
    T = P.chan("T")
    V = P.chan("V")
    SC = P.chan("SC")
    g, t, v, s, y = "gpsimd", "tensor", "vector", "scalar", "sync"
    slopes = alibi_slopes(H)

    CQ = P.chan("CQ")
    for dst, src in ((identb, IDENT), (identf, IDENT), (ones1b, ONES1),
                     (ones1f, ONES1), (onescolb, ONESCOL), (onescolf, ONESCOL),
                     (sposs, SPOS),
                     (biasblk, BIASBLK), (maskdiag, MASKDIAG), (e8b, E8),
                     (qnw, QNW), (knw, KNW), (bbias, BBIAS)):
        P.op(g, "dma_start", dst[:], src[:], inc=CQ, inck=16)
    P.op(g, "dma_start", maskr[:].rearrange("p (k c) -> p k c", c=512),
         MASKR[:].rearrange("k p c -> p k c"), inc=CQ, inck=16)
    consts_done = CQ.n

    P.waitv(v, CQ, consts_done)
    P.op(v, "memset", epsc[:, 0:1], EPS_RMS, inc=V)
    P.op(v, "memset", epsc[:, 1:2], EPS_LN, inc=V)
    P.op(v, "memset", zeroc[:], 0.0, inc=V)
    P.op(v, "scalar_tensor_tensor", wqwk[:], qnw[:], RSQ_HD, knw[:],
         op0=ALU.mult, op1=ALU.mult, inc=V)
    init_v = V.n

    XQ = P.chan("XQ")
    WQCH = [P.chan("WQ0"), P.chan("WQ1")]
    xbig_b = Buf()
    wb = [Buf(), Buf()]

    def wslice(par):
        return WBUF[:, par * DM:(par + 1) * DM]

    def load_w_head(src_ap, par, dsub=HD):
        wb[par].acquire(P, g)
        P.op(g, "dma_start",
             wslice(par).rearrange("p (a d) -> p a d", d=dsub),
             src_ap.rearrange("(a p) d -> p a d", p=128),
             inc=WQCH[par], inck=16)
        wb[par].produced(WQCH[par])

    pk_b = [Buf(), Buf()]
    pv_b = [Buf(), Buf()]
    pst_b = Buf()
    pbc_b = Buf()
    pbb_b = Buf()
    sqs_b = [Buf(), Buf()]
    invb_b = [Buf(), Buf()]
    inv1_b = Buf()
    sqbv_b = [Buf(), Buf()]
    wseq = 0

    def rms_inv_chain(pnum, sg):
        sqs_b[sg].acquire(P, s)
        P.op(s, "activation", sqs[:, sg * 512:sg * 512 + 512],
             pbank[pnum][:], AF.Square, inc=SC)
        sqs_b[sg].produced(SC)
        sqs_b[sg].wait_ready(P, t)
        pst_b.acquire(P, t)
        P.op(t, "matmul", pbank[5][0:1, :], onescolb[:],
             sqs[:, sg * 512:sg * 512 + 512], start=True, stop=True, inc=T)
        pst_b.produced(T)
        sqs_b[sg].consumed(T)
        pst_b.wait_ready(P, s)
        inv1_b.acquire(P, s)
        P.op(s, "activation", rowtmp[0:1, 0:512], pbank[5][0:1, :],
             AF.Ln, bias=epsc[0:1, 0:1], scale=1.0 / HD, inc=SC)
        P.wait(s, SC)
        P.op(s, "activation", rinv_row, rowtmp[0:1, 0:512],
             AF.Exp, bias=zeroc[0:1, 0:1], scale=-0.5, inc=SC)
        inv1_b.produced(SC)
        pst_b.consumed(SC)
        inv1_b.wait_ready(P, t)
        pbc_b.acquire(P, t)
        P.op(t, "matmul", pbank[6][:], ones1f[:], rinv_row,
             start=True, stop=True, inc=T)
        pbc_b.produced(T)
        inv1_b.consumed(T)
        pbc_b.wait_ready(P, s)
        invb_b[sg].acquire(P, s)
        P.op(s, "copy", invb_sb[:, sg * 512:sg * 512 + 512], pbank[6][:], inc=SC)
        invb_b[sg].produced(SC)
        pbc_b.consumed(SC)

    for sh in range(2):
        xbig_b.acquire(P, g)
        for fc in range(NFC):
            P.op(g, "dma_start", XBIG[:, fc * 1024:(fc + 1) * 1024],
                 XT[fc * 128:(fc + 1) * 128, sh * 1024:(sh + 1) * 1024],
                 inc=XQ, inck=16)
        xbig_b.produced(XQ)
        for kv in range(HKV):
            for wtyp, WSRC in ((0, WKT), (1, WVT)):
                par = wseq % 2
                load_w_head(WSRC[kv], par)
                wseq += 1
                for sg in range(2):
                    s0 = sh * 1024 + sg * 512
                    xbig_b.wait_ready(P, t)
                    wb[par].wait_ready(P, t)
                    P.waitv(t, V, init_v)
                    pbx = (pk_b if wtyp == 0 else pv_b)[sg]
                    pnum = (0 if wtyp == 0 else 2) + sg
                    pbx.acquire(P, t)
                    for fc in range(NFC):
                        P.op(t, "matmul", pbank[pnum][:],
                             wslice(par)[:, fc * HD:(fc + 1) * HD],
                             XBIG[:, fc * 1024 + sg * 512: fc * 1024 + sg * 512 + 512],
                             start=(fc == 0), stop=(fc == NFC - 1),
                             inc=(T if fc == NFC - 1 else None))
                    pbx.produced(T)
                    if wtyp == 0:
                        pbx.wait_ready(P, s)
                        rms_inv_chain(pnum, sg)
                        invb_b[sg].wait_ready(P, v)
                        pbx.wait_ready(P, v)
                        P.op(v, "tensor_tensor",
                             KBIG[:, kv * S + s0: kv * S + s0 + 512],
                             pbank[pnum][:], invb_sb[:, sg * 512:sg * 512 + 512],
                             op=ALU.mult, inc=V)
                        pbx.consumed(V)
                        invb_b[sg].consumed(V)
                    else:
                        pbx.wait_ready(P, v)
                        sqbv_b[sg].acquire(P, v)
                        P.op(v, "tensor_copy", SQB[:, sg * 1024:sg * 1024 + 512],
                             pbank[pnum][:], inc=V)
                        sqbv_b[sg].produced(V)
                        pbx.consumed(V)
                        sqbv_b[sg].wait_ready(P, t)
                        pbb_b.acquire(P, t)
                        for ts4 in range(4):
                            P.op(t, "transpose",
                                 pbb[:, ts4 * 128:(ts4 + 1) * 128],
                                 SQB[:, sg * 1024 + ts4 * 128: sg * 1024 + ts4 * 128 + 128],
                                 identb[:], inc=(T if ts4 == 3 else None))
                        pbb_b.produced(T)
                        sqbv_b[sg].consumed(T)
                        pbb_b.wait_ready(P, v)
                        base = (kv * 16 + s0 // 128) * 128
                        P.op(v, "tensor_copy", VBIG[:, base:base + 512],
                             pbb[:, 0:512], inc=V)
                        pbb_b.consumed(V)
                wb[par].consumed(T)
        xbig_b.consumed(T)

    kv_done_v = V.n

    QWCH = [P.chan("QW0"), P.chan("QW1")]
    if upto >= 1.5:
        xbig_b.acquire(P, g)
        for fc in range(NFC):
            P.op(g, "dma_start", XBIG[:, fc * 1024:(fc + 1) * 1024],
                 XO[fc * 128:(fc + 1) * 128, :], inc=XQ, inck=16)
        xbig_b.produced(XQ)
        stage_b = [Buf(), Buf()]
        qbs_b = [Buf(), Buf()]
        for h in range(H):
            par = wseq % 2
            load_w_head(WQT[h], par)
            wseq += 1
            hq = h % 2
            qbs_b[hq].acquire(P, s)
            for qg in range(2):
                xbig_b.wait_ready(P, t)
                wb[par].wait_ready(P, t)
                pq_b = pk_b[qg]
                pnum = qg
                pq_b.acquire(P, t)
                for fc in range(NFC):
                    P.op(t, "matmul", pbank[pnum][:],
                         wslice(par)[:, fc * HD:(fc + 1) * HD],
                         XBIG[:, fc * 1024 + qg * 512: fc * 1024 + qg * 512 + 512],
                         start=(fc == 0), stop=(fc == NFC - 1),
                         inc=(T if fc == NFC - 1 else None))
                pq_b.produced(T)
                pq_b.wait_ready(P, s)
                rms_inv_chain(pnum, qg)
                invb_b[qg].wait_ready(P, v)
                pq_b.wait_ready(P, v)
                stage_b[qg].acquire(P, v)
                P.op(v, "tensor_tensor", STAGE[:, qg * 512:qg * 512 + 512],
                     pbank[pnum][:], invb_sb[:, qg * 512:qg * 512 + 512],
                     op=ALU.mult, inc=V)
                stage_b[qg].produced(V)
                pq_b.consumed(V)
                invb_b[qg].consumed(V)
                stage_b[qg].wait_ready(P, s)
                P.op(s, "activation",
                     QBUF[:, hq * 1024 + qg * 512: hq * 1024 + qg * 512 + 512],
                     STAGE[:, qg * 512:qg * 512 + 512], AF.Copy,
                     scale=wqwk[:], bias=0.0, inc=SC)
                stage_b[qg].consumed(SC)
            qbs_b[hq].produced(SC)
            qbs_b[hq].wait_ready(P, y)
            P.op(y, "dma_start", QS[h], QBUF[:, hq * 1024:hq * 1024 + 1024],
                 inc=QWCH[hq], inck=16)
            qbs_b[hq].consumed(QWCH[hq])
            wb[par].consumed(T)
        xbig_b.consumed(T)
    qs_done = (QWCH[0].n, QWCH[1].n)

    dch = P.chan("DBG")
    if "kv" in dbg:
        P.waitv(y, V, kv_done_v)
        P.op(y, "dma_start", dbg_t["KD"][:], KBIG[:], inc=dch, inck=16)
        P.op(y, "dma_start", dbg_t["VD"][:], VBIG[:], inc=dch, inck=16)
    if "q" in dbg and upto >= 1.5:
        P.waitv(y, QWCH[0], qs_done[0])
        P.waitv(y, QWCH[1], qs_done[1])
        P.op(y, "dma_start", dbg_t["QD"][:].rearrange("d (h n) -> d h n", n=NT),
             QS[:].rearrange("h d n -> d h n"), inc=dch, inck=16)

    obig_done = 0
    if upto >= 2:
        P.waitv(t, V, kv_done_v)
        P.waitv(y, QWCH[0], qs_done[0])
        P.waitv(y, QWCH[1], qs_done[1])
        QLCH = [P.chan("QL0"), P.chan("QL1")]
        qld_b = [Buf(), Buf()]
        bm_b = [Buf(), Buf()]
        po_b = [Buf(), Buf()]
        pr_b = [Buf(), Buf()]
        pp_b = [Buf(), Buf(), Buf()]
        negUc_b = [Buf(), Buf()]
        negu_b = [Buf(), Buf()]
        negub_b = [Buf(), Buf()]
        negUb = STAGE[:, 1024:2048]
        defer_fin = []
        invr1_b = [Buf(), Buf()]
        invrb_b = [Buf(), Buf()]
        pt_b = [Buf(), Buf(), Buf(), Buf(), Buf()]
        ptslot = [PTB[:, 0:512], PTB[:, 512:1024], PTB[:, 1024:1536],
                  WBUF[:, 0:512], WBUF[:, 512:1024]]
        poolidx = 0

        def pool_next():
            nonlocal poolidx
            i = poolidx % 3
            poolidx += 1
            return pp_b[i], pbank[i]

        for kv in range(HKV):
            for gq in range(4):
                h = kv * 4 + gq
                hq = h % 2
                qld_b[hq].acquire(P, y)
                P.op(y, "dma_start", QBUF[:, hq * 1024:hq * 1024 + 1024],
                     QS[h], inc=QLCH[hq], inck=16)
                qld_b[hq].produced(QLCH[hq])
                bm_b[hq].acquire(P, v)
                P.op(v, "tensor_scalar", bm_sp[:, hq * 16:hq * 16 + 16],
                     sposs[:], float(slopes[h]), None, op0=ALU.mult)
                P.op(v, "tensor_scalar", bm_blk[:, hq * 128:hq * 128 + 128],
                     biasblk[:], float(slopes[h]), None, op0=ALU.mult, inc=V)
                bm_b[hq].produced(V)
                for m in range(2):
                    negUc_b[m].acquire(P, v)
                    for jt in range(4):
                        j = 4 * m + jt
                        L = 256 * (j + 1)
                        nch = (L + 511) // 512
                        for ci in range(nch):
                            c0 = ci * 512
                            Nc = min(512, L - c0)
                            pb_, pps = pool_next()
                            pb_.acquire(P, t)
                            qld_b[hq].wait_ready(P, t)
                            last = (ci == nch - 1)
                            P.op(t, "matmul", pps[:, 0:Nc],
                                 QBUF[:, hq * 1024 + j * 128: hq * 1024 + j * 128 + 128],
                                 KBIG[:, kv * S + c0: kv * S + c0 + Nc],
                                 start=True, stop=not last,
                                 inc=(None if last else T))
                            if last:
                                doff = (L - 256) - c0
                                P.op(t, "matmul", pps[:, doff:doff + 256],
                                     identb[:], maskdiag[:],
                                     start=False, stop=True, inc=T)
                            pb_.produced(T)
                            pb_.wait_ready(P, v)
                            P.wait(v, V)
                            nb = Nc // 16
                            P.op(v, "tensor_reduce",
                                 bmaxt[:, m * 128 + c0 // 16: m * 128 + c0 // 16 + nb],
                                 pps[:, 0:Nc].rearrange("p (a b) -> p a b", b=16),
                                 axis=AX.X, op=ALU.max, inc=V)
                            pb_.consumed(V)
                        nblk = 16 * (j + 1)
                        bm_b[hq].wait_ready(P, v)
                        P.wait(v, V)
                        P.op(v, "tensor_tensor", utmp[:, 0:nblk],
                             bmaxt[:, m * 128:m * 128 + nblk],
                             bm_blk[:, hq * 128:hq * 128 + nblk],
                             op=ALU.add, inc=V)
                        P.wait(v, V)
                        P.op(v, "tensor_reduce", negUc[:, m * 4 + jt:m * 4 + jt + 1],
                             utmp[:, 0:nblk], axis=AX.X, op=ALU.max,
                             negate=True, inc=V)
                    negUc_b[m].produced(V)
                def emit_asm(m):
                    negUc_b[m].wait_ready(P, t)
                    pbb_b.acquire(P, t)
                    for jt in range(4):
                        P.op(t, "transpose", pbb[0:1, jt * 128:jt * 128 + 128],
                             negUc[:, m * 4 + jt:m * 4 + jt + 1], identb[:],
                             inc=(T if jt == 3 else None))
                    pbb_b.produced(T)
                    negUc_b[m].consumed(T)
                    pbb_b.wait_ready(P, s)
                    negu_b[m].acquire(P, s)
                    P.op(s, "copy", negUrow[0:1, m * 512:m * 512 + 512],
                         pbb[0:1, 0:512], inc=SC)
                    negu_b[m].produced(SC)
                    pbb_b.consumed(SC)

                emit_asm(0)
                defer_fin.append(lambda: emit_asm(1))
                for m in range(2):
                    iu = m
                    St = 8 * (m + 1)
                    negu_b[m].wait_ready(P, t)
                    pb_, pps = pool_next()
                    pb_.acquire(P, t)
                    P.op(t, "matmul", pps[:], ones1b[:],
                         negUrow[0:1, m * 512:m * 512 + 512],
                         start=True, stop=True, inc=T)
                    pb_.produced(T)
                    negu_b[m].consumed(T)
                    pb_.wait_ready(P, s)
                    negub_b[m].acquire(P, s)
                    P.op(s, "copy", negUb[:, m * 512:m * 512 + 512], pps[:],
                         inc=SC)
                    negub_b[m].produced(SC)
                    pb_.consumed(SC)
                    po_b[iu].acquire(P, t)
                    pr_b[iu].acquire(P, t)
                    pend = []

                    def emit_qk(sidx, m=m, iu=iu, hq=hq, kv=kv, pend=pend):
                        kp = sidx - 8 * m
                        pb_, pps = pool_next()
                        pb_.acquire(P, t)
                        P.op(t, "matmul", pps[:],
                             KBIG[:, kv * S + sidx * 128: kv * S + sidx * 128 + 128],
                             QBUF[:, hq * 1024 + m * 512: hq * 1024 + m * 512 + 512],
                             start=True, stop=(kp < 0),
                             inc=(None if kp >= 0 else T))
                        if kp >= 0:
                            P.op(t, "matmul", pps[:], identb[:],
                                 maskr[:, kp * 512:kp * 512 + 512],
                                 start=False, stop=True, inc=T)
                        pb_.produced(T)
                        pb_.wait_ready(P, v)
                        negub_b[m].wait_ready(P, v)
                        P.op(v, "tensor_tensor", pps[:], pps[:],
                             negUb[:, m * 512:m * 512 + 512], op=ALU.add, inc=V)
                        pb_.produced(V)
                        ptb_ = pt_b[sidx % 5]
                        ptb_.acquire(P, s)
                        pb_.wait_ready(P, s)
                        bm_b[hq].wait_ready(P, s)
                        P.op(s, "activation", ptslot[sidx % 5],
                             pps[:], AF.Exp,
                             bias=bm_sp[:, hq * 16 + sidx: hq * 16 + sidx + 1],
                             scale=1.0, inc=SC)
                        ptb_.produced(SC)
                        pb_.consumed(SC)
                        pend.append((sidx, ptb_))

                    def emit_av(m=m, iu=iu, kv=kv, St=St, pend=pend):
                        sidx, ptb_ = pend.pop(0)
                        ptb_.wait_ready(P, t)
                        vb = (kv * 16 + sidx) * 128
                        P.op(t, "matmul", pbank[3 + iu][:],
                             VBIG[:, vb:vb + 128], ptslot[sidx % 5],
                             start=(sidx == 0), stop=(sidx == St - 1))
                        P.op(t, "matmul", pbank[5 + iu][0:1, :],
                             onescolb[:], ptslot[sidx % 5],
                             start=(sidx == 0), stop=(sidx == St - 1),
                             inc=(T if sidx == St - 1 else None))
                        ptb_.consumed(T)

                    for sidx in range(St):
                        emit_qk(sidx)
                        if sidx == 2:
                            while defer_fin:
                                defer_fin.pop(0)()
                        if len(pend) >= 3:
                            emit_av()
                    while pend:
                        emit_av()
                    po_b[iu].produced(T)
                    pr_b[iu].produced(T)
                    negub_b[m].consumed(V)
                    pr_b[iu].wait_ready(P, s)
                    invr1_b[iu].acquire(P, s)
                    P.op(s, "activation",
                         rowtmp[0:1, iu * 512:iu * 512 + 512],
                         pbank[5 + iu][0:1, :], AF.Ln,
                         bias=zeroc[0:1, 0:1], scale=1.0, inc=SC)
                    P.wait(s, SC)
                    P.op(s, "activation",
                         invr_sb[0:1, iu * 512:iu * 512 + 512],
                         rowtmp[0:1, iu * 512:iu * 512 + 512], AF.Exp,
                         bias=zeroc[0:1, 0:1], scale=-1.0, inc=SC)
                    invr1_b[iu].produced(SC)
                    pr_b[iu].consumed(SC)

                    def finish_om(h=h, m=m, iu=iu):
                        invr1_b[iu].wait_ready(P, t)
                        pb_, pps = pool_next()
                        pb_.acquire(P, t)
                        P.op(t, "matmul", pps[:], ones1b[:],
                             invr_sb[0:1, iu * 512:iu * 512 + 512],
                             start=True, stop=True, inc=T)
                        pb_.produced(T)
                        invr1_b[iu].consumed(T)
                        pb_.wait_ready(P, s)
                        invrb_b[iu].acquire(P, s)
                        P.op(s, "copy", invrb_sb[:, iu * 512:iu * 512 + 512],
                             pps[:], inc=SC)
                        invrb_b[iu].produced(SC)
                        pb_.consumed(SC)
                        invrb_b[iu].wait_ready(P, v)
                        po_b[iu].wait_ready(P, v)
                        P.op(v, "tensor_tensor",
                             OBIG[:, h * 1024 + m * 512: h * 1024 + m * 512 + 512],
                             pbank[3 + iu][:],
                             invrb_sb[:, iu * 512:iu * 512 + 512],
                             op=ALU.mult, inc=V)
                        po_b[iu].consumed(V)
                        invrb_b[iu].consumed(V)

                    defer_fin.append(finish_om)
                bm_b[hq].consumed(SC)
                qld_b[hq].consumed(T)
        while defer_fin:
            defer_fin.pop(0)()
        obig_done = V.n
        P.waitv(g, T, T.n)
        if "o" in dbg:
            P.waitv(y, V, obig_done)
            P.op(y, "dma_start", dbg_t["OD"][:], OBIG[:], inc=dch, inck=16)
            od_dump = dch.n

    xres_b = [Buf(), Buf()]
    stg_b = [Buf(), Buf()]
    stout_b3 = [Buf(), Buf()]
    stout_b4 = stout_b3
    XRCH = [P.chan("XR0"), P.chan("XR1")]
    SWCH = [P.chan("SW0"), P.chan("SW1")]
    x1_done_sc = 0
    acc_free_T = 0
    if upto >= 3:
        P.waitv(t, V, obig_done)
        P.waitv(t, SC, SC.n)
        psum_o = [Buf(), Buf(), Buf(), Buf()]
        sqb_q = [[Buf(), Buf()], [Buf(), Buf()]]
        pend_sq = []

        def flush_sq():
            oc_, par_, qg_ = pend_sq.pop(0)
            b = sqb_q[par_][qg_]
            b.wait_ready(P, v)
            slot = SQB[:, par_ * 1024 + qg_ * 512: par_ * 1024 + qg_ * 512 + 512]
            dst = accx2[:, qg_ * 512:qg_ * 512 + 512]
            if oc_ == 0:
                P.op(v, "tensor_copy", dst, slot, inc=V)
            else:
                P.op(v, "tensor_tensor", dst, dst, slot, op=ALU.add, inc=V)
            b.consumed(V)

        def ln_block(oc, par, qg, psum_buf, pnum, resid_stt, first_wait=0,
                     resid=None, resid_b=None):
            if resid is None:
                resid = XRES[:, par * 1024:par * 1024 + 1024]
            if resid_b is None:
                resid_b = xres_b[par]
            psum_buf.wait_ready(P, v)
            resid_b.wait_ready(P, v)
            sl = STAGE[:, par * 1024 + qg * 512: par * 1024 + qg * 512 + 512]
            if resid_stt is None:
                P.op(v, "tensor_tensor", sl, pbank[pnum][:],
                     resid[:, qg * 512:qg * 512 + 512],
                     op=ALU.add, inc=V)
            else:
                P.op(v, "scalar_tensor_tensor", sl, pbank[pnum][:],
                     resid_stt,
                     resid[:, qg * 512:qg * 512 + 512],
                     op0=ALU.add, op1=ALU.add, inc=V)
            psum_buf.consumed(V)
            if qg == 1:
                resid_b.consumed(V)
            stv = V.n
            dstx = accx[:, qg * 512:qg * 512 + 512]
            if oc == 0:
                P.waitv(v, T, first_wait)
                P.op(v, "tensor_copy", dstx, sl, inc=V)
            else:
                P.op(v, "tensor_tensor", dstx, dstx, sl, op=ALU.add, inc=V)
            sqb_q[par][qg].acquire(P, s)
            P.waitv(s, V, stv)
            P.op(s, "activation",
                 SQB[:, par * 1024 + qg * 512: par * 1024 + qg * 512 + 512],
                 sl, AF.Square, inc=SC)
            sqb_q[par][qg].produced(SC)
            pend_sq.append((oc, par, qg))
            while len(pend_sq) > 2:
                flush_sq()

        def ln_stats_reduce():
            while pend_sq:
                flush_sq()
            P.wait(t, V)
            for qg in range(2):
                P.op(t, "matmul", pbank[3 + qg][0:1, :], onescolf[:],
                     accx[:, qg * 512:qg * 512 + 512],
                     start=True, stop=True, inc=T)
                P.op(t, "matmul", pbank[5 + qg][0:1, :], onescolf[:],
                     accx2[:, qg * 512:qg * 512 + 512],
                     start=True, stop=True, inc=T)

        def ln_finalize():
            ME, ES, VA, AA, BB = 0, 512, 1024, 1536, 2048
            P.wait(s, T)
            for qg in range(2):
                P.op(s, "activation", lnrow[0:1, ME:ME + 512],
                     pbank[3 + qg][0:1, :], AF.Copy, scale=1.0 / DM,
                     bias=0.0, inc=SC)
                P.op(s, "activation", lnrow[0:1, ES:ES + 512],
                     pbank[5 + qg][0:1, :], AF.Copy, scale=1.0 / DM,
                     bias=0.0, inc=SC)
                P.wait(v, SC)
                P.op(v, "tensor_tensor", lnrow[0:1, VA:VA + 512],
                     lnrow[0:1, ME:ME + 512], lnrow[0:1, ME:ME + 512],
                     op=ALU.mult, inc=V)
                P.wait(v, V)
                P.op(v, "tensor_tensor", lnrow[0:1, VA:VA + 512],
                     lnrow[0:1, ES:ES + 512], lnrow[0:1, VA:VA + 512],
                     op=ALU.subtract, inc=V)
                P.wait(s, V)
                P.op(s, "activation", lnrow[0:1, AA:AA + 512],
                     lnrow[0:1, VA:VA + 512], AF.Ln,
                     bias=epsc[0:1, 1:2], scale=1.0, inc=SC)
                P.wait(s, SC)
                P.op(s, "activation", lnrow[0:1, AA:AA + 512],
                     lnrow[0:1, AA:AA + 512], AF.Exp,
                     bias=zeroc[0:1, 0:1], scale=-0.5, inc=SC)
                P.wait(v, SC)
                P.op(v, "tensor_tensor", lnrow[0:1, BB:BB + 512],
                     lnrow[0:1, ME:ME + 512], lnrow[0:1, AA:AA + 512],
                     op=ALU.mult, inc=V)
                P.wait(v, V)
                P.op(v, "tensor_scalar", lnrow[0:1, BB:BB + 512],
                     lnrow[0:1, BB:BB + 512], -1.0, None, op0=ALU.mult, inc=V)
                P.wait(t, V)
                P.op(t, "matmul", pbank[qg][:], ones1f[:],
                     lnrow[0:1, AA:AA + 512], start=True, stop=True)
                P.op(t, "matmul", pbank[2][:] if qg == 0 else pbank[5][:],
                     ones1f[:], lnrow[0:1, BB:BB + 512],
                     start=True, stop=True, inc=T)
                P.wait(s, T)
                P.op(s, "copy", arow[:, qg * 512:qg * 512 + 512], pbank[qg][:])
                P.op(s, "copy", brow[:, qg * 512:qg * 512 + 512],
                     pbank[2][:] if qg == 0 else pbank[5][:], inc=SC)

        for oc in range(NFC):
            par = oc % 2
            wb[par].acquire(P, g)
            P.op(g, "dma_start",
                 wslice(par).rearrange("p (a d) -> p a d", d=128),
                 WOT[oc].rearrange("a p d -> p a d"), inc=WQCH[par], inck=16)
            wb[par].produced(WQCH[par])
            xres_b[par].acquire(P, y)
            P.op(y, "dma_start", XRES[:, par * 1024:par * 1024 + 1024],
                 XO[oc * 128:(oc + 1) * 128, :], inc=XRCH[par], inck=16)
            xres_b[par].produced(XRCH[par])
            stg_b[par].acquire(P, v)
            for qg in range(2):
                pnum = 2 * (oc % 2) + qg
                pob = psum_o[pnum]
                pob.acquire(P, t)
                wb[par].wait_ready(P, t)
                for hh in range(H):
                    P.op(t, "matmul", pbank[pnum][:],
                         wslice(par)[:, hh * 128:hh * 128 + 128],
                         OBIG[:, hh * 1024 + qg * 512: hh * 1024 + qg * 512 + 512],
                         start=(hh == 0), stop=(hh == H - 1),
                         inc=(T if hh == H - 1 else None))
                pob.produced(T)
                if qg == 1:
                    wb[par].consumed(T)
                ln_block(oc, par, qg, pob, pnum, None)
            stg_b[par].produced(V)
            stg_b[par].wait_ready(P, s)
            stout_b3[par].acquire(P, s)
            P.op(s, "copy", stout3[par], STAGE[:, par * 1024:par * 1024 + 1024],
                 inc=SC)
            stout_b3[par].produced(SC)
            stg_b[par].consumed(SC)
            stout_b3[par].wait_ready(P, y)
            P.op(y, "dma_start", X1PRE[oc], stout3[par], inc=SWCH[par], inck=16)
            stout_b3[par].consumed(SWCH[par])
        if "o" in dbg:
            P.waitv(s, dch, od_dump)
        ln_stats_reduce()
        acc_free_T = T.n
        ln_finalize()
        ln1_ab = SC.n
        P.waitv(y, SWCH[0], SWCH[0].n)
        P.waitv(y, SWCH[1], SWCH[1].n)
        P.waitv(g, SWCH[0], SWCH[0].n)
        P.waitv(g, SWCH[1], SWCH[1].n)
        P.waitv(y, SC, ln1_ab)
        vb_b = [Buf() for _ in range(8)]
        VAPL = [P.chan("VAP0"), P.chan("VAP1")]

        def apply_read(oc, SRC):
            sl8 = oc % 8
            eng = y if oc % 2 == 0 else g
            ch = VAPL[oc % 2]
            vb_b[sl8].acquire(P, eng)
            P.op(eng, "dma_start", VBIG[:, sl8 * 1024:sl8 * 1024 + 1024],
                 SRC[oc], inc=ch, inck=16)
            vb_b[sl8].produced(ch)

        for oc in range(6):
            apply_read(oc, X1PRE)
        for oc in range(NFC):
            sl8 = oc % 8
            par = oc % 2
            vb_b[sl8].wait_ready(P, v)
            P.waitv(v, SC, ln1_ab)
            stg_b[par].acquire(P, v)
            P.op(v, "tensor_tensor", STAGE[:, par * 1024:par * 1024 + 1024],
                 VBIG[:, sl8 * 1024:sl8 * 1024 + 1024], arow[:], op=ALU.mult,
                 inc=V)
            P.wait(v, V)
            P.op(v, "tensor_tensor", STAGE[:, par * 1024:par * 1024 + 1024],
                 STAGE[:, par * 1024:par * 1024 + 1024], brow[:], op=ALU.add,
                 inc=V)
            stg_b[par].produced(V)
            vb_b[sl8].consumed(V)
            if oc + 6 < NFC:
                apply_read(oc + 6, X1PRE)
            stg_b[par].wait_ready(P, s)
            P.op(s, "copy", XBIG[:, oc * 1024:(oc + 1) * 1024],
                 STAGE[:, par * 1024:par * 1024 + 1024], inc=SC)
            stg_b[par].consumed(SC)
            P.waitv(y, SC, SC.n)
            P.op(y, "dma_start", X1F[oc], XBIG[:, oc * 1024:(oc + 1) * 1024],
                 inc=SWCH[par], inck=16)
        x1_done_sc = SC.n
        if "x1" in dbg:
            P.waitv(y, SWCH[0], SWCH[0].n)
            P.waitv(y, SWCH[1], SWCH[1].n)
            P.op(y, "dma_start",
                 dbg_t["X1D"][:].rearrange("a p n -> p a n"),
                 X1F[:].rearrange("a p n -> p a n"), inc=dch, inck=16)

    ODC = [P.chan("OD0"), P.chan("OD1")]
    if upto >= 4:
        P.waitv(t, SC, x1_done_sc)
        FQ = P.chan("FQ")
        P.waitv(g, V, obig_done)
        P.op(g, "dma_start", abuf.rearrange("p (a d) -> p a d", d=128),
             AT.rearrange("(a p) d -> p a d", p=128), inc=FQ, inck=16)
        P.op(g, "dma_start", bbuf, BT[:], inc=FQ, inck=16)
        P.op(g, "dma_start", wrbuf.rearrange("p (a d) -> p a d", d=8),
             WRT.rearrange("(a p) d -> p a d", p=128), inc=FQ, inck=16)
        ffnw = FQ.n
        P.waitv(v, FQ, ffnw)
        P.op(v, "tensor_scalar", bbuf, bbuf, 2.0, None, op0=ALU.mult, inc=V)
        bscale_v = V.n

        P.waitv(t, FQ, ffnw)
        for qg in range(2):
            if qg == 1:
                P.wait(t, SC)
            for fc in range(NFC):
                P.op(t, "matmul", pbank[0][0:8, :],
                     wrbuf[:, fc * 8:fc * 8 + 8],
                     XBIG[:, fc * 1024 + qg * 512: fc * 1024 + qg * 512 + 512],
                     start=(fc == 0), stop=(fc == NFC - 1),
                     inc=(T if fc == NFC - 1 else None))
            P.wait(s, T)
            P.op(s, "copy", lgsb[:, qg * 512:qg * 512 + 512], pbank[0][0:8, :],
                 inc=SC)
        P.wait(t, SC)
        for tt8 in range(8):
            P.op(t, "transpose", pbank[1][0:128, tt8 * 8:tt8 * 8 + 8],
                 lgsb[:, tt8 * 128:tt8 * 128 + 128], identf[0:8, 0:8],
                 inc=(T if tt8 == 7 else None))
        tr_done = T.n
        P.waitv(v, T, tr_done)
        for tt8 in range(8):
            if tt8 > 0:
                P.wait(v, T)
            P.op(v, "tensor_copy", rt[:, 0:8], pbank[1][0:128, tt8 * 8:tt8 * 8 + 8],
                 inc=V)
            P.wait(v, V)
            P.op(v, "reduce_max", rcol[:, 0:1], rt[:, 0:8], axis=AX.X, inc=V)
            P.wait(v, V)
            P.op(v, "tensor_scalar", rt[:, 8:16], rt[:, 0:8], rcol[:, 0:1], None,
                 op0=ALU.is_ge, inc=V)
            P.wait(v, V)
            P.op(v, "scalar_tensor_tensor", rt[:, 16:24], rt[:, 8:16], NEG_BIG,
                 rt[:, 0:8], op0=ALU.mult, op1=ALU.add, inc=V)
            P.wait(v, V)
            P.op(v, "reduce_max", rcol[:, 1:2], rt[:, 16:24], axis=AX.X, inc=V)
            P.op(v, "tensor_scalar", rcol[:, 2:3], rcol[:, 0:1], -1.0, None,
                 op0=ALU.mult, inc=V)
            P.wait(v, V)
            P.op(v, "tensor_scalar", rt[:, 24:32], rt[:, 16:24], rcol[:, 1:2],
                 None, op0=ALU.is_ge, inc=V)
            P.wait(s, V)
            P.op(s, "activation", rcol[:, 3:4], rcol[:, 1:2], AF.Exp,
                 bias=rcol[:, 2:3], scale=1.0, inc=SC)
            P.wait(v, SC)
            P.op(v, "tensor_scalar", rcol[:, 4:5], rcol[:, 3:4], 1.0, None,
                 op0=ALU.add, inc=V)
            P.wait(v, V)
            P.op(v, "reciprocal", rcol[:, 5:6], rcol[:, 4:5], inc=V)
            P.wait(v, V)
            P.op(v, "tensor_tensor", rcol[:, 6:7], rcol[:, 3:4], rcol[:, 5:6],
                 op=ALU.mult, inc=V)
            P.wait(v, V)
            P.op(v, "tensor_scalar", ctmp[:, 0:8], rt[:, 8:16], rcol[:, 5:6],
                 None, op0=ALU.mult, inc=V)
            P.wait(v, V)
            P.op(v, "scalar_tensor_tensor", ctmp[:, 0:8], rt[:, 24:32],
                 rcol[:, 6:7], ctmp[:, 0:8], op0=ALU.mult, op1=ALU.add, inc=V)
            P.wait(t, V)
            P.op(t, "transpose",
                 pbank[2 + tt8 // 4][0:8, (tt8 % 4) * 128:(tt8 % 4) * 128 + 128],
                 ctmp[:, 0:8], identf[:], inc=T)
        P.wait(s, T)
        P.op(s, "copy", cemb[:, 0:512], pbank[2][0:8, 0:512], inc=SC)
        P.op(s, "copy", cemb[:, 512:1024], pbank[3][0:8, 0:512], inc=SC)
        P.wait(t, SC)
        for qg in range(2):
            if qg == 1:
                P.wait(t, SC)
            P.op(t, "matmul", pbank[3][:], e8b[:],
                 cemb[:, qg * 512:qg * 512 + 512], start=True, stop=True, inc=T)
            P.wait(s, T)
            P.op(s, "copy", crep[:, qg * 512:qg * 512 + 512], pbank[3][:],
                 inc=SC)
        P.wait(t, SC)
        for qg in range(2):
            if qg == 1:
                P.wait(t, V)
            for fc in range(NFC):
                P.op(t, "matmul", pbank[4][:],
                     abuf[:, fc * 128:fc * 128 + 128],
                     XBIG[:, fc * 1024 + qg * 512: fc * 1024 + qg * 512 + 512],
                     start=(fc == 0), stop=(fc == NFC - 1),
                     inc=(T if fc == NFC - 1 else None))
            P.wait(v, T)
            P.waitv(v, SC, SC.n)
            P.op(v, "tensor_tensor", hcb[:, qg * 512:qg * 512 + 512],
                 pbank[4][:], crep[:, qg * 512:qg * 512 + 512], op=ALU.mult,
                 inc=V)
        hc_done = V.n

        psum_f = [Buf(), Buf(), Buf(), Buf()]
        xr4_b = [Buf(), Buf()]
        P.waitv(t, V, hc_done)
        P.waitv(t, V, bscale_v)
        P.waitv(y, SWCH[0], SWCH[0].n)
        P.waitv(y, SWCH[1], SWCH[1].n)
        for oc in range(NFC):
            par = oc % 2
            wb[par].acquire(P, g)
            P.op(g, "dma_start",
                 wslice(par).rearrange("p (a d) -> p a d", d=128),
                 WBT[oc].rearrange("a p d -> p a d"), inc=WQCH[par], inck=16)
            wb[par].produced(WQCH[par])
            xr4_b[par].acquire(P, y)
            P.op(y, "dma_start", xres4[par], X1F[oc], inc=XRCH[par], inck=16)
            xr4_b[par].produced(XRCH[par])
            stg_b[par].acquire(P, v)
            for qg in range(2):
                pnum = 2 * (oc % 2) + qg
                pfb = psum_f[pnum]
                pfb.acquire(P, t)
                wb[par].wait_ready(P, t)
                for fc in range(NFC):
                    P.op(t, "matmul", pbank[pnum][:],
                         wslice(par)[:, fc * 128:fc * 128 + 128],
                         XBIG[:, fc * 1024 + qg * 512: fc * 1024 + qg * 512 + 512],
                         start=(fc == 0), stop=False)
                P.op(t, "matmul", pbank[pnum][:],
                     bbuf[:, oc * 128:oc * 128 + 128],
                     hcb[:, qg * 512:qg * 512 + 512],
                     start=False, stop=True, inc=T)
                pfb.produced(T)
                if qg == 1:
                    wb[par].consumed(T)
                ln_block(oc, par, qg, pfb, pnum, bbias[:, oc:oc + 1],
                         first_wait=acc_free_T,
                         resid=xres4[par], resid_b=xr4_b[par])
            stg_b[par].produced(V)
            stg_b[par].wait_ready(P, s)
            stout_b4[par].acquire(P, s)
            P.op(s, "copy", stout4[par], STAGE[:, par * 1024:par * 1024 + 1024],
                 inc=SC)
            stout_b4[par].produced(SC)
            stg_b[par].consumed(SC)
            stout_b4[par].wait_ready(P, y)
            P.op(y, "dma_start", O2PRE[oc], stout4[par], inc=SWCH[par], inck=16)
            stout_b4[par].consumed(SWCH[par])
        ln_stats_reduce()
        ME, ES, VA, AA, BB = 0, 512, 1024, 1536, 2048
        P.wait(s, T)
        for qg in range(2):
            P.op(s, "activation", lnrow[0:1, ME:ME + 512],
                 pbank[3 + qg][0:1, :], AF.Copy, scale=1.0 / DM,
                 bias=0.0, inc=SC)
            P.op(s, "activation", lnrow[0:1, ES:ES + 512],
                 pbank[5 + qg][0:1, :], AF.Copy, scale=1.0 / DM,
                 bias=0.0, inc=SC)
            P.wait(v, SC)
            P.op(v, "tensor_tensor", lnrow[0:1, VA:VA + 512],
                 lnrow[0:1, ME:ME + 512], lnrow[0:1, ME:ME + 512],
                 op=ALU.mult, inc=V)
            P.wait(v, V)
            P.op(v, "tensor_tensor", lnrow[0:1, VA:VA + 512],
                 lnrow[0:1, ES:ES + 512], lnrow[0:1, VA:VA + 512],
                 op=ALU.subtract, inc=V)
            P.wait(s, V)
            P.op(s, "activation", lnrow[0:1, AA:AA + 512],
                 lnrow[0:1, VA:VA + 512], AF.Ln,
                 bias=epsc[0:1, 1:2], scale=1.0, inc=SC)
            P.wait(s, SC)
            P.op(s, "activation", lnrow[0:1, AA:AA + 512],
                 lnrow[0:1, AA:AA + 512], AF.Exp,
                 bias=zeroc[0:1, 0:1], scale=-0.5, inc=SC)
            P.wait(v, SC)
            P.op(v, "tensor_tensor", lnrow[0:1, BB:BB + 512],
                 lnrow[0:1, ME:ME + 512], lnrow[0:1, AA:AA + 512],
                 op=ALU.mult, inc=V)
            P.wait(v, V)
            P.op(v, "tensor_scalar", lnrow[0:1, BB:BB + 512],
                 lnrow[0:1, BB:BB + 512], -1.0, None, op0=ALU.mult, inc=V)
            P.wait(t, V)
            P.op(t, "matmul", pbank[qg][:], ones1f[:],
                 lnrow[0:1, AA:AA + 512], start=True, stop=True)
            P.op(t, "matmul", pbank[2][:] if qg == 0 else pbank[5][:],
                 ones1f[:], lnrow[0:1, BB:BB + 512],
                 start=True, stop=True, inc=T)
            P.wait(s, T)
            P.op(s, "copy", arow[:, qg * 512:qg * 512 + 512], pbank[qg][:])
            P.op(s, "copy", brow[:, qg * 512:qg * 512 + 512],
                 pbank[2][:] if qg == 0 else pbank[5][:], inc=SC)
        ln2_ab = SC.n
        P.waitv(y, SWCH[0], SWCH[0].n)
        P.waitv(y, SWCH[1], SWCH[1].n)
        P.waitv(g, SWCH[0], SWCH[0].n)
        P.waitv(g, SWCH[1], SWCH[1].n)
        P.waitv(y, SC, ln2_ab)
        for oc in range(6):
            apply_read(oc, O2PRE)
        for oc in range(NFC):
            sl8 = oc % 8
            par = oc % 2
            vb_b[sl8].wait_ready(P, v)
            P.waitv(v, SC, ln2_ab)
            stg_b[par].acquire(P, v)
            P.op(v, "tensor_tensor", STAGE[:, par * 1024:par * 1024 + 1024],
                 VBIG[:, sl8 * 1024:sl8 * 1024 + 1024], arow[:], op=ALU.mult,
                 inc=V)
            P.wait(v, V)
            P.op(v, "tensor_tensor", STAGE[:, par * 1024:par * 1024 + 1024],
                 STAGE[:, par * 1024:par * 1024 + 1024], brow[:], op=ALU.add,
                 inc=V)
            stg_b[par].produced(V)
            vb_b[sl8].consumed(V)
            if oc + 6 < NFC:
                apply_read(oc + 6, O2PRE)
            stg_b[par].wait_ready(P, y)
            P.op(y, "dma_start", OUT[oc * 128:(oc + 1) * 128, :],
                 STAGE[:, par * 1024:par * 1024 + 1024], inc=ODC[par], inck=16)
            stg_b[par].consumed(ODC[par])

    if "OD0" in P._chans:
        P.wait(y, P._chans["OD0"])
        P.wait(y, P._chans["OD1"])
    for chn in ("SW0", "SW1", "QW0", "QW1"):
        if chn in P._chans:
            P.wait(y, P._chans[chn])
    if dch.n:
        P.wait(y, dch)
    P.finalize()
    return nc, st




from concourse.bass_utils import run_bass_kernel_spmd

_CACHE = {}


def kernel(**inputs):
    if "nc" not in _CACHE:
        _CACHE["nc"] = build(upto=4, debug=())
    nc, _st = _CACHE["nc"]
    in_maps = [stage_inputs(inputs, c) for c in range(8)]
    res = run_bass_kernel_spmd(nc, in_maps, core_ids=list(range(8)))
    out = np.empty((4, S, DM), dtype=np.float32)
    for c in range(8):
        b, half = c // 2, c % 2
        out[b, half::2, :] = np.asarray(res.results[c]["OUT"]).T
    return out


# version 10
# speedup vs baseline: 1.1058x; 1.0199x over previous
import math
import numpy as np
from contextlib import ExitStack

import concourse.bass as bass
import concourse.mybir as mybir

F32 = mybir.dt.float32
BF16 = mybir.dt.bfloat16
AF = mybir.ActivationFunctionType
ALU = mybir.AluOpType
AX = mybir.AxisListType

DM = 4096
S = 2048
NT = 1024
H = 32
HKV = 8
HD = 128
NFC = DM // 128
EPS_RMS = 1e-6
EPS_LN = 1e-5
NEG_BIG = -1e30
RSQ_HD = 1.0 / math.sqrt(HD)


def alibi_slopes(n_heads):
    start = 2 ** (-(2 ** (-(math.log2(n_heads) - 3))))
    return [start * start ** i for i in range(n_heads)]


def host_consts(half):
    c = {}
    c["IDENT"] = np.eye(128, dtype=np.float32)
    c["ONES1"] = np.ones((1, 128), dtype=np.float32)
    c["ONESCOL"] = np.ones((128, 1), dtype=np.float32)
    c["SPOS"] = (np.arange(128, dtype=np.float32)[:, None]
                 + 128.0 * np.arange(16, dtype=np.float32)[None, :])
    c["BIASBLK"] = np.broadcast_to(
        16.0 * np.arange(128, dtype=np.float32) + 15.0, (128, 128)).copy()
    ql = np.arange(128)[:, None]
    t = np.arange(256)[None, :]
    c["MASKDIAG"] = np.where(t > 2 * ql + half, NEG_BIG, 0.0).astype(np.float32)
    sl = np.arange(128)[:, None]
    jq = np.arange(512)[None, :]
    c["MASKR"] = np.stack([
        np.where(128 * k + sl > 2 * jq + half, NEG_BIG, 0.0) for k in range(8)
    ]).astype(np.float32)
    er = np.arange(128)[None, :]
    e = np.arange(8)[:, None]
    c["E8"] = (er // 16 == e).astype(np.float32)
    return c


def stage_inputs(inputs, core):
    b, half = core // 2, core % 2
    x = np.asarray(inputs["L0_attn_x"])
    Wq = np.asarray(inputs["L0_attn_Wq"])
    Wk = np.asarray(inputs["L0_attn_Wk"])
    Wv = np.asarray(inputs["L0_attn_Wv"])
    Wo = np.asarray(inputs["L0_attn_Wo"])
    qw = np.asarray(inputs["L0_attn_q_norm_w"])
    kw = np.asarray(inputs["L0_attn_k_norm_w"])
    Wb = np.asarray(inputs["L0_ffn_W_base"])
    bb = np.asarray(inputs["L0_ffn_b_base"])
    Wr = np.asarray(inputs["L0_ffn_W_router"])
    A = np.asarray(inputs["L0_ffn_A_stack"])
    B = np.asarray(inputs["L0_ffn_B_stack"])

    d = {}
    xt = np.ascontiguousarray(x[b].T)
    d["XT"] = xt
    d["XO"] = np.ascontiguousarray(xt[:, half::2])
    d["WQT"] = np.ascontiguousarray(Wq.T.reshape(DM, H, HD).transpose(1, 0, 2))
    d["WKT"] = np.ascontiguousarray(Wk.T.reshape(DM, HKV, HD).transpose(1, 0, 2))
    d["WVT"] = np.ascontiguousarray(Wv.T.reshape(DM, HKV, HD).transpose(1, 0, 2))
    d["WOT"] = np.ascontiguousarray(
        Wo.T.reshape(H, HD, NFC, 128).transpose(2, 0, 1, 3))
    d["WBT"] = np.ascontiguousarray(
        Wb.T.reshape(NFC, 128, NFC, 128).transpose(2, 0, 1, 3))
    d["WRT"] = np.ascontiguousarray(Wr.T)
    d["AT"] = np.ascontiguousarray(A.reshape(128, DM).T)
    d["BT"] = np.ascontiguousarray(B.transpose(0, 2, 1).reshape(128, DM))
    d["BBIAS"] = np.ascontiguousarray(bb.reshape(NFC, 128).T)
    d["QNW"] = np.ascontiguousarray(qw.reshape(128, 1).astype(np.float32))
    d["KNW"] = np.ascontiguousarray(kw.reshape(128, 1).astype(np.float32))
    d.update(host_consts(half))
    return d


class Chan:
    def __init__(self, st, nc, name):
        self.sem = st.enter_context(nc.semaphore(name))
        self.n = 0
        self.lastw = {}


class Buf:
    def __init__(self):
        self.ready = None
        self.free = None

    def acquire(self, P, eng):
        if self.free is not None:
            P.waitv(eng, *self.free)

    def produced(self, chan):
        self.ready = (chan, chan.n)

    def wait_ready(self, P, eng):
        if self.ready is not None:
            P.waitv(eng, *self.ready)

    def consumed(self, chan):
        self.free = (chan, chan.n)


class Prog:
    ENGS = ("tensor", "vector", "scalar", "sync", "gpsimd")

    def __init__(self, nc, st):
        self.nc = nc
        self.st = st
        self.ops = {e: [] for e in self.ENGS}
        self._chans = {}

    def chan(self, name):
        if name not in self._chans:
            self._chans[name] = Chan(self.st, self.nc, name)
        return self._chans[name]

    def waitv(self, eng, chan, v):
        if v <= 0 or chan.lastw.get(eng, -1) >= v:
            return
        chan.lastw[eng] = v
        sem = chan.sem
        self.ops[eng].append(lambda e, s=sem, vv=v: e.wait_ge(s, vv))

    def wait(self, eng, chan):
        self.waitv(eng, chan, chan.n)

    def op(self, eng, meth, *args, inc=None, inck=1, **kwargs):
        if inc is not None:
            inc.n += inck
            sem = inc.sem
            def f(e, m=meth, a=args, kw=kwargs, s=sem, k=inck):
                getattr(e, m)(*a, **kw).then_inc(s, k)
        else:
            def f(e, m=meth, a=args, kw=kwargs):
                getattr(e, m)(*a, **kw)
        self.ops[eng].append(f)

    def finalize(self):
        with self.nc.Block() as block:
            @block.tensor
            def _(t):
                for f in self.ops["tensor"]:
                    f(t)

            @block.vector
            def _(v):
                for f in self.ops["vector"]:
                    f(v)

            @block.scalar
            def _(s):
                for f in self.ops["scalar"]:
                    f(s)

            @block.sync
            def _(s):
                for f in self.ops["sync"]:
                    f(s)

            @block.gpsimd
            def _(g):
                for f in self.ops["gpsimd"]:
                    f(g)


def build(upto=4, debug=()):
    nc = bass.Bass("TRN2", target_bir_lowering=False)
    st = ExitStack()
    dbg = set(debug)

    di = lambda name, shape: nc.dram_tensor(name, shape, F32, kind="ExternalInput")
    XT = di("XT", [DM, S])
    XO = di("XO", [DM, NT])
    WQT = di("WQT", [H, DM, HD])
    WKT = di("WKT", [HKV, DM, HD])
    WVT = di("WVT", [HKV, DM, HD])
    WOT = di("WOT", [NFC, H, HD, 128])
    WBT = di("WBT", [NFC, NFC, 128, 128])
    WRT = di("WRT", [DM, 8])
    AT = di("AT", [DM, 128])
    BT = di("BT", [128, DM])
    BBIAS = di("BBIAS", [128, NFC])
    QNW = di("QNW", [128, 1])
    KNW = di("KNW", [128, 1])
    IDENT = di("IDENT", [128, 128])
    ONES1 = di("ONES1", [1, 128])
    ONESCOL = di("ONESCOL", [128, 1])
    SPOS = di("SPOS", [128, 16])
    BIASBLK = di("BIASBLK", [128, 128])
    MASKDIAG = di("MASKDIAG", [128, 256])
    MASKR = di("MASKR", [8, 128, 512])
    E8 = di("E8", [8, 128])

    OUT = nc.dram_tensor("OUT", [DM, NT], F32, kind="ExternalOutput")
    QS = nc.dram_tensor("QS", [H, HD, NT], BF16, kind="Internal")
    X1PRE = nc.dram_tensor("X1PRE", [NFC, 128, NT], BF16, kind="Internal")
    X1F = nc.dram_tensor("X1F", [NFC, 128, NT], BF16, kind="Internal")
    O2PRE = nc.dram_tensor("O2PRE", [NFC, 128, NT], BF16, kind="Internal")

    dbg_t = {}
    if "kv" in dbg:
        dbg_t["KD"] = nc.dram_tensor("KD", [HD, HKV * S], BF16, kind="ExternalOutput")
        dbg_t["VD"] = nc.dram_tensor("VD", [128, HKV * 16 * HD], BF16, kind="ExternalOutput")
    if "q" in dbg:
        dbg_t["QD"] = nc.dram_tensor("QD", [HD, H * NT], BF16, kind="ExternalOutput")
    if "o" in dbg:
        dbg_t["OD"] = nc.dram_tensor("OD", [HD, H * NT], BF16, kind="ExternalOutput")
    if "x1" in dbg:
        dbg_t["X1D"] = nc.dram_tensor("X1D", [NFC, 128, NT], F32, kind="ExternalOutput")
    if "p" in dbg:
        dbg_t["PD"] = nc.dram_tensor("PD", [8, 128, 512], BF16, kind="ExternalOutput")
        dbg_t["UD"] = nc.dram_tensor("UD", [1, 512], BF16, kind="ExternalOutput")

    sb = lambda name, shape, dt: st.enter_context(nc.sbuf_tensor(name, shape, dt))
    XBIG = sb("XBIG", [128, NFC * 1024], BF16)
    OBIG = XBIG
    KBIG = sb("KBIG", [128, HKV * S], BF16)
    VBIG = sb("VBIG", [128, HKV * 16 * HD], BF16)
    WBUF = sb("WBUF", [128, 2 * DM], BF16)
    QBUF = sb("QBUF", [128, 2 * 1024], BF16)
    PTB = sb("PTB", [128, 3 * 512], BF16)
    XRES = sb("XRES", [128, 2 * 1024], F32)
    STAGE = sb("STAGE", [128, 2 * 1024], F32)
    SQB = sb("SQB", [128, 2 * 1024], BF16)

    identb = sb("identb", [128, 128], BF16)
    identf = sb("identf", [128, 128], F32)
    ones1b = sb("ones1b", [1, 128], BF16)
    ones1f = sb("ones1f", [1, 128], F32)
    onescolb = sb("onescolb", [128, 1], BF16)
    onescolf = sb("onescolf", [128, 1], F32)
    sposs = sb("sposs", [128, 16], F32)
    biasblk = sb("biasblk", [128, 128], F32)
    maskdiag = sb("maskdiag", [128, 256], BF16)
    maskr = sb("maskr", [128, 8 * 512], BF16)
    e8b = sb("e8b", [8, 128], BF16)
    qnw = sb("qnw", [128, 1], F32)
    knw = sb("knw", [128, 1], F32)
    wqwk = sb("wqwk", [128, 1], F32)
    bbias = sb("bbias", [128, NFC], F32)
    epsc = sb("epsc", [128, 2], F32)
    zeroc = sb("zeroc", [128, 1], F32)
    bm_sp = sb("bm_sp", [128, 2 * 16], F32)
    bm_blk = sb("bm_blk", [128, 2 * 128], F32)
    bmaxt = sb("bmaxt", [128, 2 * 128], F32)
    utmp = sb("utmp", [128, 128], F32)
    negUc = sb("negUc", [128, 8], BF16)
    sqs = sb("sqs", [128, 2 * 512], BF16)
    lnrow = sb("lnrow", [1, 5 * 512], F32)
    arow = sb("arow", [128, 1024], F32)
    brow = sb("brow", [128, 1024], F32)
    rt = sb("rt", [128, 32], F32)
    rcol = sb("rcol", [128, 8], F32)
    ctmp = sb("ctmp", [128, 8], F32)
    negUrow = SQB[0:1, 0:1024]
    invr_sb = SQB[0:1, 1024:2048]
    invrb_sb = sqs
    rowtmp = lnrow
    rinv_row = lnrow[0:1, 1024:1536]
    invb_sb = arow
    abuf = KBIG[:, 0:DM]
    bbuf = KBIG[:, DM:2 * DM]
    wrbuf = KBIG[:, 2 * DM:2 * DM + NFC * 8]
    xres4 = [KBIG[:, 8448:9472], KBIG[:, 9472:10496]]
    stout3 = [PTB[:, 0:1024], QBUF[:, 1024:2048]]
    stout4 = stout3
    lgsb = brow[0:8, :]
    crep = arow
    accx = arow
    accx2 = brow
    cemb = SQB[0:8, 0:1024]
    hcb = QBUF[:, 0:1024]

    pbank = [st.enter_context(nc.psum_tensor(f"pb{i}", [128, 512], F32))
             for i in range(7)]
    pbb = st.enter_context(nc.psum_tensor("pbb", [128, 1024], BF16))

    P = Prog(nc, st)
    T = P.chan("T")
    V = P.chan("V")
    SC = P.chan("SC")
    g, t, v, s, y = "gpsimd", "tensor", "vector", "scalar", "sync"
    slopes = alibi_slopes(H)

    CQ = P.chan("CQ")
    for dst, src in ((identb, IDENT), (identf, IDENT), (ones1b, ONES1),
                     (ones1f, ONES1), (onescolb, ONESCOL), (onescolf, ONESCOL),
                     (sposs, SPOS),
                     (biasblk, BIASBLK), (maskdiag, MASKDIAG), (e8b, E8),
                     (qnw, QNW), (knw, KNW), (bbias, BBIAS)):
        P.op(g, "dma_start", dst[:], src[:], inc=CQ, inck=16)
    P.op(g, "dma_start", maskr[:].rearrange("p (k c) -> p k c", c=512),
         MASKR[:].rearrange("k p c -> p k c"), inc=CQ, inck=16)
    consts_done = CQ.n

    P.waitv(v, CQ, consts_done)
    P.op(v, "memset", epsc[:, 0:1], EPS_RMS, inc=V)
    P.op(v, "memset", epsc[:, 1:2], EPS_LN, inc=V)
    P.op(v, "memset", zeroc[:], 0.0, inc=V)
    P.op(v, "scalar_tensor_tensor", wqwk[:], qnw[:], RSQ_HD, knw[:],
         op0=ALU.mult, op1=ALU.mult, inc=V)
    init_v = V.n

    XQ = P.chan("XQ")
    WQCH = [P.chan("WQ0"), P.chan("WQ1")]
    xbig_b = Buf()
    wb = [Buf(), Buf()]

    def wslice(par):
        return WBUF[:, par * DM:(par + 1) * DM]

    def load_w_head(src_ap, par, dsub=HD):
        wb[par].acquire(P, g)
        P.op(g, "dma_start",
             wslice(par).rearrange("p (a d) -> p a d", d=dsub),
             src_ap.rearrange("(a p) d -> p a d", p=128),
             inc=WQCH[par], inck=16)
        wb[par].produced(WQCH[par])

    pk_b = [Buf(), Buf()]
    pv_b = [Buf(), Buf()]
    pst_b = Buf()
    pbc_b = Buf()
    pbb_b = Buf()
    sqs_b = [Buf(), Buf()]
    invb_b = [Buf(), Buf()]
    inv1_b = Buf()
    sqbv_b = [Buf(), Buf()]
    wseq = 0

    def rms_inv_chain(pnum, sg):
        sqs_b[sg].acquire(P, s)
        P.op(s, "activation", sqs[:, sg * 512:sg * 512 + 512],
             pbank[pnum][:], AF.Square, inc=SC)
        sqs_b[sg].produced(SC)
        sqs_b[sg].wait_ready(P, t)
        pst_b.acquire(P, t)
        P.op(t, "matmul", pbank[5][0:1, :], onescolb[:],
             sqs[:, sg * 512:sg * 512 + 512], start=True, stop=True, inc=T)
        pst_b.produced(T)
        sqs_b[sg].consumed(T)
        pst_b.wait_ready(P, s)
        inv1_b.acquire(P, s)
        P.op(s, "activation", rowtmp[0:1, 0:512], pbank[5][0:1, :],
             AF.Ln, bias=epsc[0:1, 0:1], scale=1.0 / HD, inc=SC)
        P.wait(s, SC)
        P.op(s, "activation", rinv_row, rowtmp[0:1, 0:512],
             AF.Exp, bias=zeroc[0:1, 0:1], scale=-0.5, inc=SC)
        inv1_b.produced(SC)
        pst_b.consumed(SC)
        inv1_b.wait_ready(P, t)
        pbc_b.acquire(P, t)
        P.op(t, "matmul", pbank[6][:], ones1f[:], rinv_row,
             start=True, stop=True, inc=T)
        pbc_b.produced(T)
        inv1_b.consumed(T)
        pbc_b.wait_ready(P, s)
        invb_b[sg].acquire(P, s)
        P.op(s, "copy", invb_sb[:, sg * 512:sg * 512 + 512], pbank[6][:], inc=SC)
        invb_b[sg].produced(SC)
        pbc_b.consumed(SC)

    for sh in range(2):
        xbig_b.acquire(P, g)
        for fc in range(NFC):
            P.op(g, "dma_start", XBIG[:, fc * 1024:(fc + 1) * 1024],
                 XT[fc * 128:(fc + 1) * 128, sh * 1024:(sh + 1) * 1024],
                 inc=XQ, inck=16)
        xbig_b.produced(XQ)
        for kv in range(HKV):
            for wtyp, WSRC in ((0, WKT), (1, WVT)):
                par = wseq % 2
                load_w_head(WSRC[kv], par)
                wseq += 1
                for sg in range(2):
                    s0 = sh * 1024 + sg * 512
                    xbig_b.wait_ready(P, t)
                    wb[par].wait_ready(P, t)
                    P.waitv(t, V, init_v)
                    pbx = (pk_b if wtyp == 0 else pv_b)[sg]
                    pnum = (0 if wtyp == 0 else 2) + sg
                    pbx.acquire(P, t)
                    for fc in range(NFC):
                        P.op(t, "matmul", pbank[pnum][:],
                             wslice(par)[:, fc * HD:(fc + 1) * HD],
                             XBIG[:, fc * 1024 + sg * 512: fc * 1024 + sg * 512 + 512],
                             start=(fc == 0), stop=(fc == NFC - 1),
                             inc=(T if fc == NFC - 1 else None))
                    pbx.produced(T)
                    if wtyp == 0:
                        pbx.wait_ready(P, s)
                        rms_inv_chain(pnum, sg)
                        invb_b[sg].wait_ready(P, v)
                        pbx.wait_ready(P, v)
                        P.op(v, "tensor_tensor",
                             KBIG[:, kv * S + s0: kv * S + s0 + 512],
                             pbank[pnum][:], invb_sb[:, sg * 512:sg * 512 + 512],
                             op=ALU.mult, inc=V)
                        pbx.consumed(V)
                        invb_b[sg].consumed(V)
                    else:
                        pbx.wait_ready(P, v)
                        sqbv_b[sg].acquire(P, v)
                        P.op(v, "tensor_copy", SQB[:, sg * 1024:sg * 1024 + 512],
                             pbank[pnum][:], inc=V)
                        sqbv_b[sg].produced(V)
                        pbx.consumed(V)
                        sqbv_b[sg].wait_ready(P, t)
                        pbb_b.acquire(P, t)
                        for ts4 in range(4):
                            P.op(t, "transpose",
                                 pbb[:, ts4 * 128:(ts4 + 1) * 128],
                                 SQB[:, sg * 1024 + ts4 * 128: sg * 1024 + ts4 * 128 + 128],
                                 identb[:], inc=(T if ts4 == 3 else None))
                        pbb_b.produced(T)
                        sqbv_b[sg].consumed(T)
                        pbb_b.wait_ready(P, v)
                        base = (kv * 16 + s0 // 128) * 128
                        P.op(v, "tensor_copy", VBIG[:, base:base + 512],
                             pbb[:, 0:512], inc=V)
                        pbb_b.consumed(V)
                wb[par].consumed(T)
        xbig_b.consumed(T)

    kv_done_v = V.n

    QWCH = [P.chan("QW0"), P.chan("QW1")]
    if upto >= 1.5:
        xbig_b.acquire(P, g)
        for fc in range(NFC):
            P.op(g, "dma_start", XBIG[:, fc * 1024:(fc + 1) * 1024],
                 XO[fc * 128:(fc + 1) * 128, :], inc=XQ, inck=16)
        xbig_b.produced(XQ)
        stage_b = [Buf(), Buf()]
        qbs_b = [Buf(), Buf()]
        for h in range(H):
            par = wseq % 2
            load_w_head(WQT[h], par)
            wseq += 1
            hq = h % 2
            qbs_b[hq].acquire(P, s)
            for qg in range(2):
                xbig_b.wait_ready(P, t)
                wb[par].wait_ready(P, t)
                pq_b = pk_b[qg]
                pnum = qg
                pq_b.acquire(P, t)
                for fc in range(NFC):
                    P.op(t, "matmul", pbank[pnum][:],
                         wslice(par)[:, fc * HD:(fc + 1) * HD],
                         XBIG[:, fc * 1024 + qg * 512: fc * 1024 + qg * 512 + 512],
                         start=(fc == 0), stop=(fc == NFC - 1),
                         inc=(T if fc == NFC - 1 else None))
                pq_b.produced(T)
                pq_b.wait_ready(P, s)
                rms_inv_chain(pnum, qg)
                invb_b[qg].wait_ready(P, v)
                pq_b.wait_ready(P, v)
                stage_b[qg].acquire(P, v)
                P.op(v, "tensor_tensor", STAGE[:, qg * 512:qg * 512 + 512],
                     pbank[pnum][:], invb_sb[:, qg * 512:qg * 512 + 512],
                     op=ALU.mult, inc=V)
                stage_b[qg].produced(V)
                pq_b.consumed(V)
                invb_b[qg].consumed(V)
                stage_b[qg].wait_ready(P, s)
                P.op(s, "activation",
                     QBUF[:, hq * 1024 + qg * 512: hq * 1024 + qg * 512 + 512],
                     STAGE[:, qg * 512:qg * 512 + 512], AF.Copy,
                     scale=wqwk[:], bias=0.0, inc=SC)
                stage_b[qg].consumed(SC)
            qbs_b[hq].produced(SC)
            qbs_b[hq].wait_ready(P, y)
            P.op(y, "dma_start", QS[h], QBUF[:, hq * 1024:hq * 1024 + 1024],
                 inc=QWCH[hq], inck=16)
            qbs_b[hq].consumed(QWCH[hq])
            wb[par].consumed(T)
        xbig_b.consumed(T)
    qs_done = (QWCH[0].n, QWCH[1].n)

    dch = P.chan("DBG")
    if "kv" in dbg:
        P.waitv(y, V, kv_done_v)
        P.op(y, "dma_start", dbg_t["KD"][:], KBIG[:], inc=dch, inck=16)
        P.op(y, "dma_start", dbg_t["VD"][:], VBIG[:], inc=dch, inck=16)
    if "q" in dbg and upto >= 1.5:
        P.waitv(y, QWCH[0], qs_done[0])
        P.waitv(y, QWCH[1], qs_done[1])
        P.op(y, "dma_start", dbg_t["QD"][:].rearrange("d (h n) -> d h n", n=NT),
             QS[:].rearrange("h d n -> d h n"), inc=dch, inck=16)

    obig_done = 0
    if upto >= 2:
        P.waitv(t, V, kv_done_v)
        P.waitv(y, QWCH[0], qs_done[0])
        P.waitv(y, QWCH[1], qs_done[1])
        QLCH = [P.chan("QL0"), P.chan("QL1")]
        qld_b = [Buf(), Buf()]
        bm_b = [Buf(), Buf()]
        po_b = [Buf(), Buf()]
        pr_b = [Buf(), Buf()]
        pp_b = [Buf(), Buf(), Buf()]
        negUc_b = [Buf(), Buf()]
        negu_b = [Buf(), Buf()]
        negub_b = [Buf(), Buf()]
        negUb = STAGE[:, 1024:2048]
        defer_fin = []
        invr1_b = [Buf(), Buf()]
        invrb_b = [Buf(), Buf()]
        pt_b = [Buf(), Buf(), Buf(), Buf(), Buf()]
        ptslot = [PTB[:, 0:512], PTB[:, 512:1024], PTB[:, 1024:1536],
                  WBUF[:, 0:512], WBUF[:, 512:1024]]
        poolidx = 0

        def pool_next():
            nonlocal poolidx
            i = poolidx % 3
            poolidx += 1
            return pp_b[i], pbank[i]

        for kv in range(HKV):
            for gq in range(4):
                h = kv * 4 + gq
                hq = h % 2
                qld_b[hq].acquire(P, y)
                P.op(y, "dma_start", QBUF[:, hq * 1024:hq * 1024 + 1024],
                     QS[h], inc=QLCH[hq], inck=16)
                qld_b[hq].produced(QLCH[hq])
                bm_b[hq].acquire(P, v)
                P.op(v, "tensor_scalar", bm_sp[:, hq * 16:hq * 16 + 16],
                     sposs[:], float(slopes[h]), None, op0=ALU.mult)
                P.op(v, "tensor_scalar", bm_blk[:, hq * 128:hq * 128 + 128],
                     biasblk[:], float(slopes[h]), None, op0=ALU.mult, inc=V)
                bm_b[hq].produced(V)
                for m in range(2):
                    negUc_b[m].acquire(P, v)
                    for jt in range(4):
                        j = 4 * m + jt
                        L = 256 * (j + 1)
                        nch = (L + 511) // 512
                        for ci in range(nch):
                            c0 = ci * 512
                            Nc = min(512, L - c0)
                            pb_, pps = pool_next()
                            pb_.acquire(P, t)
                            qld_b[hq].wait_ready(P, t)
                            last = (ci == nch - 1)
                            P.op(t, "matmul", pps[:, 0:Nc],
                                 QBUF[:, hq * 1024 + j * 128: hq * 1024 + j * 128 + 128],
                                 KBIG[:, kv * S + c0: kv * S + c0 + Nc],
                                 start=True, stop=not last,
                                 inc=(None if last else T))
                            if last:
                                doff = (L - 256) - c0
                                P.op(t, "matmul", pps[:, doff:doff + 256],
                                     identb[:], maskdiag[:],
                                     start=False, stop=True, inc=T)
                            pb_.produced(T)
                            pb_.wait_ready(P, v)
                            P.wait(v, V)
                            nb = Nc // 16
                            P.op(v, "tensor_reduce",
                                 bmaxt[:, m * 128 + c0 // 16: m * 128 + c0 // 16 + nb],
                                 pps[:, 0:Nc].rearrange("p (a b) -> p a b", b=16),
                                 axis=AX.X, op=ALU.max, inc=V)
                            pb_.consumed(V)
                        nblk = 16 * (j + 1)
                        bm_b[hq].wait_ready(P, v)
                        P.wait(v, V)
                        P.op(v, "tensor_tensor", utmp[:, 0:nblk],
                             bmaxt[:, m * 128:m * 128 + nblk],
                             bm_blk[:, hq * 128:hq * 128 + nblk],
                             op=ALU.add, inc=V)
                        P.wait(v, V)
                        P.op(v, "tensor_reduce", negUc[:, m * 4 + jt:m * 4 + jt + 1],
                             utmp[:, 0:nblk], axis=AX.X, op=ALU.max,
                             negate=True, inc=V)
                    negUc_b[m].produced(V)
                def emit_asm(m):
                    negUc_b[m].wait_ready(P, t)
                    pbb_b.acquire(P, t)
                    for jt in range(4):
                        P.op(t, "transpose", pbb[0:1, jt * 128:jt * 128 + 128],
                             negUc[:, m * 4 + jt:m * 4 + jt + 1], identb[:],
                             inc=(T if jt == 3 else None))
                    pbb_b.produced(T)
                    negUc_b[m].consumed(T)
                    pbb_b.wait_ready(P, s)
                    negu_b[m].acquire(P, s)
                    P.op(s, "copy", negUrow[0:1, m * 512:m * 512 + 512],
                         pbb[0:1, 0:512], inc=SC)
                    negu_b[m].produced(SC)
                    pbb_b.consumed(SC)

                emit_asm(0)
                defer_fin.append(lambda: emit_asm(1))
                for m in range(2):
                    iu = m
                    St = 8 * (m + 1)
                    negu_b[m].wait_ready(P, t)
                    pb_, pps = pool_next()
                    pb_.acquire(P, t)
                    P.op(t, "matmul", pps[:], ones1b[:],
                         negUrow[0:1, m * 512:m * 512 + 512],
                         start=True, stop=True, inc=T)
                    pb_.produced(T)
                    negu_b[m].consumed(T)
                    pb_.wait_ready(P, s)
                    negub_b[m].acquire(P, s)
                    P.op(s, "copy", negUb[:, m * 512:m * 512 + 512], pps[:],
                         inc=SC)
                    negub_b[m].produced(SC)
                    pb_.consumed(SC)
                    po_b[iu].acquire(P, t)
                    pr_b[iu].acquire(P, t)
                    pend = []

                    def emit_qk(sidx, m=m, iu=iu, hq=hq, kv=kv, pend=pend):
                        kp = sidx - 8 * m
                        pb_, pps = pool_next()
                        pb_.acquire(P, t)
                        P.op(t, "matmul", pps[:],
                             KBIG[:, kv * S + sidx * 128: kv * S + sidx * 128 + 128],
                             QBUF[:, hq * 1024 + m * 512: hq * 1024 + m * 512 + 512],
                             start=True, stop=(kp < 0),
                             inc=(None if kp >= 0 else T))
                        if kp >= 0:
                            P.op(t, "matmul", pps[:], identb[:],
                                 maskr[:, kp * 512:kp * 512 + 512],
                                 start=False, stop=True, inc=T)
                        pb_.produced(T)
                        pb_.wait_ready(P, v)
                        negub_b[m].wait_ready(P, v)
                        P.op(v, "tensor_tensor", pps[:], pps[:],
                             negUb[:, m * 512:m * 512 + 512], op=ALU.add, inc=V)
                        pb_.produced(V)
                        ptb_ = pt_b[sidx % 5]
                        ptb_.acquire(P, s)
                        pb_.wait_ready(P, s)
                        bm_b[hq].wait_ready(P, s)
                        P.op(s, "activation", ptslot[sidx % 5],
                             pps[:], AF.Exp,
                             bias=bm_sp[:, hq * 16 + sidx: hq * 16 + sidx + 1],
                             scale=1.0, inc=SC)
                        ptb_.produced(SC)
                        pb_.consumed(SC)
                        pend.append((sidx, ptb_))

                    def emit_av(m=m, iu=iu, kv=kv, St=St, pend=pend):
                        sidx, ptb_ = pend.pop(0)
                        ptb_.wait_ready(P, t)
                        vb = (kv * 16 + sidx) * 128
                        P.op(t, "matmul", pbank[3 + iu][:],
                             VBIG[:, vb:vb + 128], ptslot[sidx % 5],
                             start=(sidx == 0), stop=(sidx == St - 1))
                        P.op(t, "matmul", pbank[5 + iu][0:1, :],
                             onescolb[:], ptslot[sidx % 5],
                             start=(sidx == 0), stop=(sidx == St - 1),
                             inc=(T if sidx == St - 1 else None))
                        ptb_.consumed(T)

                    for sidx in range(St):
                        emit_qk(sidx)
                        if sidx == 2:
                            while defer_fin:
                                defer_fin.pop(0)()
                        if len(pend) >= 3:
                            emit_av()
                    while pend:
                        emit_av()
                    po_b[iu].produced(T)
                    pr_b[iu].produced(T)
                    negub_b[m].consumed(V)
                    pr_b[iu].wait_ready(P, s)
                    invr1_b[iu].acquire(P, s)
                    P.op(s, "activation",
                         rowtmp[0:1, iu * 512:iu * 512 + 512],
                         pbank[5 + iu][0:1, :], AF.Ln,
                         bias=zeroc[0:1, 0:1], scale=1.0, inc=SC)
                    P.wait(s, SC)
                    P.op(s, "activation",
                         invr_sb[0:1, iu * 512:iu * 512 + 512],
                         rowtmp[0:1, iu * 512:iu * 512 + 512], AF.Exp,
                         bias=zeroc[0:1, 0:1], scale=-1.0, inc=SC)
                    invr1_b[iu].produced(SC)
                    pr_b[iu].consumed(SC)

                    def finish_om(h=h, m=m, iu=iu):
                        invr1_b[iu].wait_ready(P, t)
                        pb_, pps = pool_next()
                        pb_.acquire(P, t)
                        P.op(t, "matmul", pps[:], ones1b[:],
                             invr_sb[0:1, iu * 512:iu * 512 + 512],
                             start=True, stop=True, inc=T)
                        pb_.produced(T)
                        invr1_b[iu].consumed(T)
                        pb_.wait_ready(P, s)
                        invrb_b[iu].acquire(P, s)
                        P.op(s, "copy", invrb_sb[:, iu * 512:iu * 512 + 512],
                             pps[:], inc=SC)
                        invrb_b[iu].produced(SC)
                        pb_.consumed(SC)
                        invrb_b[iu].wait_ready(P, v)
                        po_b[iu].wait_ready(P, v)
                        P.op(v, "tensor_tensor",
                             OBIG[:, h * 1024 + m * 512: h * 1024 + m * 512 + 512],
                             pbank[3 + iu][:],
                             invrb_sb[:, iu * 512:iu * 512 + 512],
                             op=ALU.mult, inc=V)
                        po_b[iu].consumed(V)
                        invrb_b[iu].consumed(V)

                    defer_fin.append(finish_om)
                bm_b[hq].consumed(SC)
                qld_b[hq].consumed(T)
        while defer_fin:
            defer_fin.pop(0)()
        obig_done = V.n
        P.waitv(g, T, T.n)
        if "o" in dbg:
            P.waitv(y, V, obig_done)
            P.op(y, "dma_start", dbg_t["OD"][:], OBIG[:], inc=dch, inck=16)
            od_dump = dch.n

    xres_b = [Buf(), Buf()]
    stg_b = [Buf(), Buf()]
    stout_b3 = [Buf(), Buf()]
    stout_b4 = stout_b3
    XRCH = [P.chan("XR0"), P.chan("XR1")]
    SWCH = [P.chan("SW0"), P.chan("SW1")]
    x1_done_sc = 0
    acc_free_T = 0
    if upto >= 3:
        P.waitv(t, V, obig_done)
        P.waitv(t, SC, SC.n)
        psum_o = [Buf(), Buf(), Buf(), Buf()]
        sqb_q = [[Buf(), Buf()], [Buf(), Buf()]]
        pend_sq = []

        def flush_sq():
            oc_, par_, qg_ = pend_sq.pop(0)
            b = sqb_q[par_][qg_]
            b.wait_ready(P, v)
            slot = SQB[:, par_ * 1024 + qg_ * 512: par_ * 1024 + qg_ * 512 + 512]
            dst = accx2[:, qg_ * 512:qg_ * 512 + 512]
            if oc_ == 0:
                P.op(v, "tensor_copy", dst, slot, inc=V)
            else:
                P.op(v, "tensor_tensor", dst, dst, slot, op=ALU.add, inc=V)
            b.consumed(V)

        def ln_block(oc, par, qg, psum_buf, pnum, resid_stt, first_wait=0,
                     resid=None, resid_b=None):
            if resid is None:
                resid = XRES[:, par * 1024:par * 1024 + 1024]
            if resid_b is None:
                resid_b = xres_b[par]
            psum_buf.wait_ready(P, v)
            resid_b.wait_ready(P, v)
            sl = STAGE[:, par * 1024 + qg * 512: par * 1024 + qg * 512 + 512]
            if resid_stt is None:
                P.op(v, "tensor_tensor", sl, pbank[pnum][:],
                     resid[:, qg * 512:qg * 512 + 512],
                     op=ALU.add, inc=V)
            else:
                P.op(v, "scalar_tensor_tensor", sl, pbank[pnum][:],
                     resid_stt,
                     resid[:, qg * 512:qg * 512 + 512],
                     op0=ALU.add, op1=ALU.add, inc=V)
            psum_buf.consumed(V)
            if qg == 1:
                resid_b.consumed(V)
            stv = V.n
            dstx = accx[:, qg * 512:qg * 512 + 512]
            if oc == 0:
                P.waitv(v, T, first_wait)
                P.op(v, "tensor_copy", dstx, sl, inc=V)
            else:
                P.op(v, "tensor_tensor", dstx, dstx, sl, op=ALU.add, inc=V)
            sqb_q[par][qg].acquire(P, s)
            P.waitv(s, V, stv)
            P.op(s, "activation",
                 SQB[:, par * 1024 + qg * 512: par * 1024 + qg * 512 + 512],
                 sl, AF.Square, inc=SC)
            sqb_q[par][qg].produced(SC)
            pend_sq.append((oc, par, qg))
            while len(pend_sq) > 2:
                flush_sq()

        def ln_stats_reduce():
            while pend_sq:
                flush_sq()
            P.wait(t, V)
            for qg in range(2):
                P.op(t, "matmul", pbank[3 + qg][0:1, :], onescolf[:],
                     accx[:, qg * 512:qg * 512 + 512],
                     start=True, stop=True, inc=T)
                P.op(t, "matmul", pbank[5 + qg][0:1, :], onescolf[:],
                     accx2[:, qg * 512:qg * 512 + 512],
                     start=True, stop=True, inc=T)

        def ln_finalize():
            ME, ES, VA, AA, BB = 0, 512, 1024, 1536, 2048
            P.wait(s, T)
            for qg in range(2):
                P.op(s, "activation", lnrow[0:1, ME:ME + 512],
                     pbank[3 + qg][0:1, :], AF.Copy, scale=1.0 / DM,
                     bias=0.0, inc=SC)
                P.op(s, "activation", lnrow[0:1, ES:ES + 512],
                     pbank[5 + qg][0:1, :], AF.Copy, scale=1.0 / DM,
                     bias=0.0, inc=SC)
                P.wait(v, SC)
                P.op(v, "tensor_tensor", lnrow[0:1, VA:VA + 512],
                     lnrow[0:1, ME:ME + 512], lnrow[0:1, ME:ME + 512],
                     op=ALU.mult, inc=V)
                P.wait(v, V)
                P.op(v, "tensor_tensor", lnrow[0:1, VA:VA + 512],
                     lnrow[0:1, ES:ES + 512], lnrow[0:1, VA:VA + 512],
                     op=ALU.subtract, inc=V)
                P.wait(s, V)
                P.op(s, "activation", lnrow[0:1, AA:AA + 512],
                     lnrow[0:1, VA:VA + 512], AF.Ln,
                     bias=epsc[0:1, 1:2], scale=1.0, inc=SC)
                P.wait(s, SC)
                P.op(s, "activation", lnrow[0:1, AA:AA + 512],
                     lnrow[0:1, AA:AA + 512], AF.Exp,
                     bias=zeroc[0:1, 0:1], scale=-0.5, inc=SC)
                P.wait(v, SC)
                P.op(v, "tensor_tensor", lnrow[0:1, BB:BB + 512],
                     lnrow[0:1, ME:ME + 512], lnrow[0:1, AA:AA + 512],
                     op=ALU.mult, inc=V)
                P.wait(v, V)
                P.op(v, "tensor_scalar", lnrow[0:1, BB:BB + 512],
                     lnrow[0:1, BB:BB + 512], -1.0, None, op0=ALU.mult, inc=V)
                P.wait(t, V)
                P.op(t, "matmul", pbank[qg][:], ones1f[:],
                     lnrow[0:1, AA:AA + 512], start=True, stop=True)
                P.op(t, "matmul", pbank[2][:] if qg == 0 else pbank[5][:],
                     ones1f[:], lnrow[0:1, BB:BB + 512],
                     start=True, stop=True, inc=T)
                P.wait(s, T)
                P.op(s, "copy", arow[:, qg * 512:qg * 512 + 512], pbank[qg][:])
                P.op(s, "copy", brow[:, qg * 512:qg * 512 + 512],
                     pbank[2][:] if qg == 0 else pbank[5][:], inc=SC)

        for oc in range(NFC):
            par = oc % 2
            wb[par].acquire(P, g)
            P.op(g, "dma_start",
                 wslice(par).rearrange("p (a d) -> p a d", d=128),
                 WOT[oc].rearrange("a p d -> p a d"), inc=WQCH[par], inck=16)
            wb[par].produced(WQCH[par])
            xres_b[par].acquire(P, y)
            P.op(y, "dma_start", XRES[:, par * 1024:par * 1024 + 1024],
                 XO[oc * 128:(oc + 1) * 128, :], inc=XRCH[par], inck=16)
            xres_b[par].produced(XRCH[par])
            stg_b[par].acquire(P, v)
            for qg in range(2):
                pnum = 2 * (oc % 2) + qg
                pob = psum_o[pnum]
                pob.acquire(P, t)
                wb[par].wait_ready(P, t)
                for hh in range(H):
                    P.op(t, "matmul", pbank[pnum][:],
                         wslice(par)[:, hh * 128:hh * 128 + 128],
                         OBIG[:, hh * 1024 + qg * 512: hh * 1024 + qg * 512 + 512],
                         start=(hh == 0), stop=(hh == H - 1),
                         inc=(T if hh == H - 1 else None))
                pob.produced(T)
                if qg == 1:
                    wb[par].consumed(T)
                ln_block(oc, par, qg, pob, pnum, None)
            stg_b[par].produced(V)
            stg_b[par].wait_ready(P, s)
            stout_b3[par].acquire(P, s)
            P.op(s, "copy", stout3[par], STAGE[:, par * 1024:par * 1024 + 1024],
                 inc=SC)
            stout_b3[par].produced(SC)
            stg_b[par].consumed(SC)
            stout_b3[par].wait_ready(P, y)
            P.op(y, "dma_start", X1PRE[oc], stout3[par], inc=SWCH[par], inck=16)
            stout_b3[par].consumed(SWCH[par])
        if "o" in dbg:
            P.waitv(s, dch, od_dump)
        FQ = P.chan("FQ")
        P.op(g, "dma_start", abuf.rearrange("p (a d) -> p a d", d=128),
             AT.rearrange("(a p) d -> p a d", p=128), inc=FQ, inck=16)
        P.op(g, "dma_start", bbuf, BT[:], inc=FQ, inck=16)
        P.op(g, "dma_start", wrbuf.rearrange("p (a d) -> p a d", d=8),
             WRT.rearrange("(a p) d -> p a d", p=128), inc=FQ, inck=16)
        ffnw = FQ.n
        ln_stats_reduce()
        acc_free_T = T.n
        ln_finalize()
        ln1_ab = SC.n
        P.waitv(y, SWCH[0], SWCH[0].n)
        P.waitv(y, SWCH[1], SWCH[1].n)
        P.waitv(g, SWCH[0], SWCH[0].n)
        P.waitv(g, SWCH[1], SWCH[1].n)
        P.waitv(y, SC, ln1_ab)
        vb_b = [Buf() for _ in range(8)]
        VAPL = [P.chan("VAP0"), P.chan("VAP1")]

        def apply_read(oc, SRC):
            sl8 = oc % 8
            eng = y if oc % 2 == 0 else g
            ch = VAPL[oc % 2]
            vb_b[sl8].acquire(P, eng)
            P.op(eng, "dma_start", VBIG[:, sl8 * 1024:sl8 * 1024 + 1024],
                 SRC[oc], inc=ch, inck=16)
            vb_b[sl8].produced(ch)

        for oc in range(6):
            apply_read(oc, X1PRE)
        for oc in range(NFC):
            sl8 = oc % 8
            par = oc % 2
            vb_b[sl8].wait_ready(P, v)
            P.waitv(v, SC, ln1_ab)
            stg_b[par].acquire(P, v)
            P.op(v, "tensor_tensor", STAGE[:, par * 1024:par * 1024 + 1024],
                 VBIG[:, sl8 * 1024:sl8 * 1024 + 1024], arow[:], op=ALU.mult,
                 inc=V)
            P.wait(v, V)
            P.op(v, "tensor_tensor", STAGE[:, par * 1024:par * 1024 + 1024],
                 STAGE[:, par * 1024:par * 1024 + 1024], brow[:], op=ALU.add,
                 inc=V)
            stg_b[par].produced(V)
            vb_b[sl8].consumed(V)
            if oc + 6 < NFC:
                apply_read(oc + 6, X1PRE)
            stg_b[par].wait_ready(P, s)
            P.op(s, "copy", XBIG[:, oc * 1024:(oc + 1) * 1024],
                 STAGE[:, par * 1024:par * 1024 + 1024], inc=SC)
            stg_b[par].consumed(SC)
            P.waitv(y, SC, SC.n)
            P.op(y, "dma_start", X1F[oc], XBIG[:, oc * 1024:(oc + 1) * 1024],
                 inc=SWCH[par], inck=16)
            if oc == 0:
                P.waitv(t, FQ, ffnw)
                P.waitv(t, SC, ln1_ab)
            P.wait(t, SC)
            for qg in range(2):
                P.op(t, "matmul", pbank[qg][0:8, :],
                     wrbuf[:, oc * 8:oc * 8 + 8],
                     XBIG[:, oc * 1024 + qg * 512: oc * 1024 + qg * 512 + 512],
                     start=(oc == 0), stop=(oc == NFC - 1),
                     inc=(T if oc == NFC - 1 else None))
                P.op(t, "matmul", pbank[2 + qg][:],
                     abuf[:, oc * 128:oc * 128 + 128],
                     XBIG[:, oc * 1024 + qg * 512: oc * 1024 + qg * 512 + 512],
                     start=(oc == 0), stop=(oc == NFC - 1),
                     inc=(T if oc == NFC - 1 else None))
        x1_done_sc = SC.n
        rtA_T = T.n
        if "x1" in dbg:
            P.waitv(y, SWCH[0], SWCH[0].n)
            P.waitv(y, SWCH[1], SWCH[1].n)
            P.op(y, "dma_start",
                 dbg_t["X1D"][:].rearrange("a p n -> p a n"),
                 X1F[:].rearrange("a p n -> p a n"), inc=dch, inck=16)

    ODC = [P.chan("OD0"), P.chan("OD1")]
    if upto >= 4:
        P.waitv(t, SC, x1_done_sc)
        P.waitv(v, FQ, ffnw)
        P.op(v, "tensor_scalar", bbuf, bbuf, 2.0, None, op0=ALU.mult, inc=V)
        bscale_v = V.n

        P.waitv(s, T, rtA_T)
        P.op(s, "copy", lgsb[:, 0:512], pbank[0][0:8, :], inc=SC)
        P.op(s, "copy", lgsb[:, 512:1024], pbank[1][0:8, :], inc=SC)
        P.wait(t, SC)
        for tt8 in range(8):
            P.op(t, "transpose", pbank[4][0:128, tt8 * 8:tt8 * 8 + 8],
                 lgsb[:, tt8 * 128:tt8 * 128 + 128], identf[0:8, 0:8],
                 inc=(T if tt8 == 7 else None))
        RT = bm_blk[:, 0:64]
        X2 = bm_blk[:, 64:128]
        M1 = bm_blk[:, 128:192]
        M2 = bm_blk[:, 192:256]
        G1 = bmaxt[:, 0:8]
        G2 = bmaxt[:, 8:16]
        D8 = bmaxt[:, 16:24]
        R8 = bmaxt[:, 24:32]
        RC = bmaxt[:, 32:40]
        C2 = bmaxt[:, 40:48]
        TD = bmaxt[:, 48:56]
        gr = lambda ap: ap.rearrange("p (a r) -> p a r", r=8)
        g1 = lambda ap: ap.rearrange("p (a r) -> p a r", r=1)
        P.wait(v, T)
        P.op(v, "tensor_copy", RT, pbank[4][0:128, 0:64], inc=V)
        P.wait(v, V)
        P.op(v, "tensor_reduce", G1, gr(RT), axis=AX.X, op=ALU.max, inc=V)
        P.wait(v, V)
        a_, b_ = bass.broadcast_tensor_aps(gr(RT), g1(G1))
        P.op(v, "tensor_tensor", gr(M1), a_, b_, op=ALU.is_ge, inc=V)
        P.wait(v, V)
        P.op(v, "scalar_tensor_tensor", X2, M1, NEG_BIG, RT,
             op0=ALU.mult, op1=ALU.add, inc=V)
        P.wait(v, V)
        P.op(v, "tensor_reduce", G2, gr(X2), axis=AX.X, op=ALU.max, inc=V)
        P.wait(v, V)
        a_, b_ = bass.broadcast_tensor_aps(gr(X2), g1(G2))
        P.op(v, "tensor_tensor", gr(M2), a_, b_, op=ALU.is_ge, inc=V)
        P.op(v, "tensor_tensor", TD, G2, G1, op=ALU.subtract, inc=V)
        P.wait(s, V)
        P.op(s, "activation", D8, TD, AF.Exp, bias=0.0, scale=1.0, inc=SC)
        P.wait(v, SC)
        P.op(v, "tensor_scalar", R8, D8, 1.0, None, op0=ALU.add, inc=V)
        P.wait(v, V)
        P.op(v, "reciprocal", RC, R8, inc=V)
        P.wait(v, V)
        P.op(v, "tensor_tensor", C2, D8, RC, op=ALU.mult, inc=V)
        P.wait(v, V)
        a_, b_ = bass.broadcast_tensor_aps(gr(M1), g1(RC))
        P.op(v, "tensor_tensor", gr(M1), a_, b_, op=ALU.mult, inc=V)
        a_, b_ = bass.broadcast_tensor_aps(gr(M2), g1(C2))
        P.op(v, "tensor_tensor", gr(M2), a_, b_, op=ALU.mult, inc=V)
        P.wait(v, V)
        P.op(v, "tensor_tensor", X2, M1, M2, op=ALU.add, inc=V)
        cmb_v = V.n
        P.waitv(t, V, cmb_v)
        for tt8 in range(8):
            P.op(t, "transpose",
                 pbank[5 + tt8 // 4][0:8, (tt8 % 4) * 128:(tt8 % 4) * 128 + 128],
                 X2[:, tt8 * 8:tt8 * 8 + 8], identf[:],
                 inc=(T if tt8 == 7 else None))
        P.wait(s, T)
        P.op(s, "copy", cemb[:, 0:512], pbank[5][0:8, 0:512], inc=SC)
        P.op(s, "copy", cemb[:, 512:1024], pbank[6][0:8, 0:512], inc=SC)
        P.wait(t, SC)
        for qg in range(2):
            if qg == 1:
                P.wait(t, SC)
            P.op(t, "matmul", pbank[4][:], e8b[:],
                 cemb[:, qg * 512:qg * 512 + 512], start=True, stop=True, inc=T)
            P.wait(s, T)
            P.op(s, "copy", crep[:, qg * 512:qg * 512 + 512], pbank[4][:],
                 inc=SC)
        P.waitv(v, T, rtA_T)
        P.waitv(v, SC, SC.n)
        P.op(v, "tensor_tensor", hcb[:, 0:512], pbank[2][:],
             crep[:, 0:512], op=ALU.mult, inc=V)
        P.op(v, "tensor_tensor", hcb[:, 512:1024], pbank[3][:],
             crep[:, 512:1024], op=ALU.mult, inc=V)
        hc_done = V.n

        psum_f = [Buf(), Buf(), Buf(), Buf()]
        xr4_b = [Buf(), Buf()]
        P.waitv(t, V, hc_done)
        P.waitv(t, V, bscale_v)
        P.waitv(y, SWCH[0], SWCH[0].n)
        P.waitv(y, SWCH[1], SWCH[1].n)
        for oc in range(NFC):
            par = oc % 2
            wb[par].acquire(P, g)
            P.op(g, "dma_start",
                 wslice(par).rearrange("p (a d) -> p a d", d=128),
                 WBT[oc].rearrange("a p d -> p a d"), inc=WQCH[par], inck=16)
            wb[par].produced(WQCH[par])
            xr4_b[par].acquire(P, y)
            P.op(y, "dma_start", xres4[par], X1F[oc], inc=XRCH[par], inck=16)
            xr4_b[par].produced(XRCH[par])
            stg_b[par].acquire(P, v)
            for qg in range(2):
                pnum = 2 * (oc % 2) + qg
                pfb = psum_f[pnum]
                pfb.acquire(P, t)
                wb[par].wait_ready(P, t)
                for fc in range(NFC):
                    P.op(t, "matmul", pbank[pnum][:],
                         wslice(par)[:, fc * 128:fc * 128 + 128],
                         XBIG[:, fc * 1024 + qg * 512: fc * 1024 + qg * 512 + 512],
                         start=(fc == 0), stop=False)
                P.op(t, "matmul", pbank[pnum][:],
                     bbuf[:, oc * 128:oc * 128 + 128],
                     hcb[:, qg * 512:qg * 512 + 512],
                     start=False, stop=True, inc=T)
                pfb.produced(T)
                if qg == 1:
                    wb[par].consumed(T)
                ln_block(oc, par, qg, pfb, pnum, bbias[:, oc:oc + 1],
                         first_wait=acc_free_T,
                         resid=xres4[par], resid_b=xr4_b[par])
            stg_b[par].produced(V)
            stg_b[par].wait_ready(P, s)
            stout_b4[par].acquire(P, s)
            P.op(s, "copy", stout4[par], STAGE[:, par * 1024:par * 1024 + 1024],
                 inc=SC)
            stout_b4[par].produced(SC)
            stg_b[par].consumed(SC)
            stout_b4[par].wait_ready(P, y)
            P.op(y, "dma_start", O2PRE[oc], stout4[par], inc=SWCH[par], inck=16)
            stout_b4[par].consumed(SWCH[par])
        ln_stats_reduce()
        ME, ES, VA, AA, BB = 0, 512, 1024, 1536, 2048
        P.wait(s, T)
        for qg in range(2):
            P.op(s, "activation", lnrow[0:1, ME:ME + 512],
                 pbank[3 + qg][0:1, :], AF.Copy, scale=1.0 / DM,
                 bias=0.0, inc=SC)
            P.op(s, "activation", lnrow[0:1, ES:ES + 512],
                 pbank[5 + qg][0:1, :], AF.Copy, scale=1.0 / DM,
                 bias=0.0, inc=SC)
            P.wait(v, SC)
            P.op(v, "tensor_tensor", lnrow[0:1, VA:VA + 512],
                 lnrow[0:1, ME:ME + 512], lnrow[0:1, ME:ME + 512],
                 op=ALU.mult, inc=V)
            P.wait(v, V)
            P.op(v, "tensor_tensor", lnrow[0:1, VA:VA + 512],
                 lnrow[0:1, ES:ES + 512], lnrow[0:1, VA:VA + 512],
                 op=ALU.subtract, inc=V)
            P.wait(s, V)
            P.op(s, "activation", lnrow[0:1, AA:AA + 512],
                 lnrow[0:1, VA:VA + 512], AF.Ln,
                 bias=epsc[0:1, 1:2], scale=1.0, inc=SC)
            P.wait(s, SC)
            P.op(s, "activation", lnrow[0:1, AA:AA + 512],
                 lnrow[0:1, AA:AA + 512], AF.Exp,
                 bias=zeroc[0:1, 0:1], scale=-0.5, inc=SC)
            P.wait(v, SC)
            P.op(v, "tensor_tensor", lnrow[0:1, BB:BB + 512],
                 lnrow[0:1, ME:ME + 512], lnrow[0:1, AA:AA + 512],
                 op=ALU.mult, inc=V)
            P.wait(v, V)
            P.op(v, "tensor_scalar", lnrow[0:1, BB:BB + 512],
                 lnrow[0:1, BB:BB + 512], -1.0, None, op0=ALU.mult, inc=V)
            P.wait(t, V)
            P.op(t, "matmul", pbank[qg][:], ones1f[:],
                 lnrow[0:1, AA:AA + 512], start=True, stop=True)
            P.op(t, "matmul", pbank[2][:] if qg == 0 else pbank[5][:],
                 ones1f[:], lnrow[0:1, BB:BB + 512],
                 start=True, stop=True, inc=T)
            P.wait(s, T)
            P.op(s, "copy", arow[:, qg * 512:qg * 512 + 512], pbank[qg][:])
            P.op(s, "copy", brow[:, qg * 512:qg * 512 + 512],
                 pbank[2][:] if qg == 0 else pbank[5][:], inc=SC)
        ln2_ab = SC.n
        P.waitv(y, SWCH[0], SWCH[0].n)
        P.waitv(y, SWCH[1], SWCH[1].n)
        P.waitv(g, SWCH[0], SWCH[0].n)
        P.waitv(g, SWCH[1], SWCH[1].n)
        P.waitv(y, SC, ln2_ab)
        for oc in range(6):
            apply_read(oc, O2PRE)
        for oc in range(NFC):
            sl8 = oc % 8
            par = oc % 2
            vb_b[sl8].wait_ready(P, v)
            P.waitv(v, SC, ln2_ab)
            stg_b[par].acquire(P, v)
            P.op(v, "tensor_tensor", STAGE[:, par * 1024:par * 1024 + 1024],
                 VBIG[:, sl8 * 1024:sl8 * 1024 + 1024], arow[:], op=ALU.mult,
                 inc=V)
            P.wait(v, V)
            P.op(v, "tensor_tensor", STAGE[:, par * 1024:par * 1024 + 1024],
                 STAGE[:, par * 1024:par * 1024 + 1024], brow[:], op=ALU.add,
                 inc=V)
            stg_b[par].produced(V)
            vb_b[sl8].consumed(V)
            if oc + 6 < NFC:
                apply_read(oc + 6, O2PRE)
            stg_b[par].wait_ready(P, y)
            P.op(y, "dma_start", OUT[oc * 128:(oc + 1) * 128, :],
                 STAGE[:, par * 1024:par * 1024 + 1024], inc=ODC[par], inck=16)
            stg_b[par].consumed(ODC[par])

    if "OD0" in P._chans:
        P.wait(y, P._chans["OD0"])
        P.wait(y, P._chans["OD1"])
    for chn in ("SW0", "SW1", "QW0", "QW1"):
        if chn in P._chans:
            P.wait(y, P._chans[chn])
    if dch.n:
        P.wait(y, dch)
    P.finalize()
    return nc, st




from concourse.bass_utils import run_bass_kernel_spmd

_CACHE = {}


def kernel(**inputs):
    if "nc" not in _CACHE:
        _CACHE["nc"] = build(upto=4, debug=())
    nc, _st = _CACHE["nc"]
    in_maps = [stage_inputs(inputs, c) for c in range(8)]
    res = run_bass_kernel_spmd(nc, in_maps, core_ids=list(range(8)))
    out = np.empty((4, S, DM), dtype=np.float32)
    for c in range(8):
        b, half = c // 2, c % 2
        out[b, half::2, :] = np.asarray(res.results[c]["OUT"]).T
    return out
